# version 34
# baseline (speedup 1.0000x reference)
import numpy as np
import ml_dtypes
from contextlib import ExitStack

import concourse.bass as bass
import concourse.mybir as mybir
from concourse.bass_utils import run_bass_kernel_spmd

F32 = mybir.dt.float32
BF16 = mybir.dt.bfloat16
I8 = mybir.dt.int8
AX = mybir.AxisListType
ALU = mybir.AluOpType
ACT = mybir.ActivationFunctionType

T = 4096
DM = 2048
H = 16
HD = 128
INTER = 5632
G2 = 2 * INTER
NT = 32
NOWN = 8
EPS = 1e-5
CR = 12582912.0
DSC = float(HD) ** -0.5
NC_A, NC_B = 16, 32
NFI = INTER // 128
NPAIR = INTER // 512
N_GW = DM * G2
N_DW = INTER * DM

ENGINES = ("sync", "gpsimd", "tensor", "vector", "scalar")


class Sem:
    __slots__ = ("h", "n")
    def __init__(self, h):
        self.h = h
        self.n = 0


class B:
    def __init__(self, nc, es):
        self.steps = {e: [] for e in ENGINES}
        self.nc = nc
        self._es = es

    def sem(self, name):
        return Sem(self._es.enter_context(self.nc.semaphore(name)))

    def _wl(self, waits):
        out = []
        for w in waits:
            if isinstance(w, tuple):
                s, v = w
                out.append((s.h, s.n if v is None else v))
            else:
                out.append((w.h, w.n))
        return out

    def op(self, engine, fn, waits=(), inc=None, inc_n=1, drain=False):
        wl = self._wl(waits)
        if inc is not None:
            inc.n += inc_n
        ih = inc.h if inc is not None else None
        def f(eng, wl=wl, fn=fn, ih=ih, inc_n=inc_n, drain=drain):
            for h, v in wl:
                if v > 0:
                    eng.wait_ge(h, v)
            if drain:
                eng.drain()
            ins = fn(eng)
            if ih is not None:
                ins.then_inc(ih, inc_n)
        self.steps[engine].append(f)
        return inc.n if inc is not None else None

    def dma(self, engine, out, in_, sem, waits=()):
        return self.op(engine, lambda e, o=out, i=in_: e.dma_start(out=o, in_=i),
                       waits=waits, inc=sem, inc_n=16)

    def wait(self, engine, *waits):
        wl = self._wl(waits)
        def f(eng, wl=wl):
            for h, v in wl:
                if v > 0:
                    eng.wait_ge(h, v)
        self.steps[engine].append(f)

    def barrier(self, bar):
        target = bar.n + len(ENGINES)
        bar.n = target
        for e in ENGINES:
            def f(eng, h=bar.h, t=target):
                eng.sem_inc(h, 1)
                eng.wait_ge(h, t)
            self.steps[e].append(f)


def build_nc():
    nc = bass.Bass("TRN2", target_bir_lowering=False, debug=False)

    xfull = nc.dram_tensor("xfull", [T, DM], F32, kind="ExternalInput").ap()
    xown = nc.dram_tensor("xown", [NOWN * 128, DM], F32, kind="ExternalInput").ap()
    wqt = nc.dram_tensor("wqt", [DM, DM], F32, kind="ExternalInput").ap()
    wkt = nc.dram_tensor("wkt", [DM, DM], F32, kind="ExternalInput").ap()
    wvt = nc.dram_tensor("wvt", [DM, DM], F32, kind="ExternalInput").ap()
    wot_r = nc.dram_tensor("wot_r", [128, 4 * 16 * 512], F32,
                           kind="ExternalInput").ap()
    gwt_r = nc.dram_tensor("gwt_r", [128, 176 * 1024], F32,
                           kind="ExternalInput").ap()
    dwt_r = nc.dram_tensor("dwt_r", [128, 88 * 1024], F32,
                           kind="ExternalInput").ap()
    gbg = nc.dram_tensor("gbg", [128, DM], BF16, kind="ExternalInput").ap()
    gbb = nc.dram_tensor("gbb", [128, DM], BF16, kind="ExternalInput").ap()
    gnwin = nc.dram_tensor("gnwin", [128, DM], F32, kind="ExternalInput").ap()
    dnwin = nc.dram_tensor("dnwin", [128, INTER], F32, kind="ExternalInput").ap()
    identin = nc.dram_tensor("identin", [128, 128], BF16, kind="ExternalInput").ap()
    masksin = nc.dram_tensor("masksin", [NT, 128, 512], BF16, kind="ExternalInput").ap()
    outd = nc.dram_tensor("outd", [NOWN * 128, DM], F32, kind="ExternalOutput").ap()
    ksc = nc.dram_tensor("ksc", [H, 128, NT, 128], BF16, kind="Internal").ap()
    vsc = nc.dram_tensor("vsc", [H, 128, NT, 128], BF16, kind="Internal").ap()
    qsc = nc.dram_tensor("qsc", [H, 128, NOWN, 128], BF16, kind="Internal").ap()
    xqsc = nc.dram_tensor("xqsc", [NOWN, 128, 16, 128], BF16, kind="Internal").ap()
    hgscr = nc.dram_tensor("hgscr", [NOWN, 128, INTER], F32, kind="Internal").ap()
    hqtd = nc.dram_tensor("hqtd", [NOWN, 128, NFI, 128], BF16, kind="Internal").ap()

    es = ExitStack()
    E = es.enter_context
    b = B(nc, es)

    bar = b.sem("bar")
    s_x = [b.sem("s_x0"), b.sem("s_x1")]
    s_w = b.sem("s_w")
    s_abs = [b.sem(f"s_abs{i}") for i in range(3)]
    s_kvw = [b.sem("s_kvw0"), b.sem("s_kvw1")]
    s_ktb = [b.sem("s_ktb0"), b.sem("s_ktb1")]
    s_kv = [b.sem("s_kv0"), b.sem("s_kv1")]
    s_q3 = [b.sem("s_q30"), b.sem("s_q31")]
    s_msk = b.sem("s_msk")
    s_wo = [b.sem("s_wo0"), b.sem("s_wo1")]
    s_gw = [b.sem(f"s_gw{i}") for i in range(3)]
    s_xq = [b.sem(f"s_xq{i}") for i in range(3)]
    s_hgw = [b.sem("s_hgw0"), b.sem("s_hgw1")]
    s_hgr = [b.sem(f"s_hgr{i}") for i in range(4)]
    s_hqw = [b.sem("s_hqw0"), b.sem("s_hqw1")]
    s_hqt = [b.sem("s_hqt0"), b.sem("s_hqt1")]
    s_xo = [b.sem("s_xo0"), b.sem("s_xo1")]
    s_out = [b.sem("s_out0"), b.sem("s_out1")]
    s_misc = b.sem("s_misc")
    cV = b.sem("cV")
    cS = b.sem("cS")
    cT = b.sem("cT")
    CTR = {"cV": cV, "cS": cS, "cT": cT}

    def V(fn, waits=(), drain=True):
        w = list(waits) + ([(cV, cV.n)] if drain else [])
        return b.op("vector", fn, waits=w, inc=cV)

    def SC(fn, waits=(), drain=True):
        w = list(waits) + ([(cS, cS.n)] if drain else [])
        return b.op("scalar", fn, waits=w, inc=cS)

    def TE(fn, waits=(), inc=True):
        return b.op("tensor", fn, waits=waits, inc=(cT if inc else None))

    consts = E(nc.sbuf_tensor("consts", [128, 8], F32))
    ones_bf = E(nc.sbuf_tensor("ones_bf", [128, 128], BF16))
    ones1f = E(nc.sbuf_tensor("ones1f", [1, 128], F32))
    onesrf = E(nc.sbuf_tensor("onesrf", [128, 1], F32))
    ident = E(nc.sbuf_tensor("ident", [128, 128], BF16))
    wabs_g = E(nc.sbuf_tensor("wabs_g", [128, 176], F32))
    wabs_d = E(nc.sbuf_tensor("wabs_d", [128, 88], F32))
    wabs_r = E(nc.sbuf_tensor("wabs_r", [128, 32], F32))
    wabs_s = E(nc.sbuf_tensor("wabs_s", [128, 1], F32))
    acc_tmp = [E(nc.sbuf_tensor(f"acc_tmp{i}", [128, 1], F32)) for i in range(2)]
    thg_p = E(nc.sbuf_tensor("thg_p", [128, 1], F32))
    thg_m = E(nc.sbuf_tensor("thg_m", [128, 1], F32))
    thd_p = E(nc.sbuf_tensor("thd_p", [128, 1], F32))
    thd_m = E(nc.sbuf_tensor("thd_m", [128, 1], F32))
    hmg = E(nc.sbuf_tensor("hmg", [128, 1], F32))
    hmd = E(nc.sbuf_tensor("hmd", [128, 1], F32))
    rs_g = E(nc.sbuf_tensor("rs_g", [128, 8], F32))
    rsg2 = E(nc.sbuf_tensor("rsg2", [128, 8], F32))
    rsg2u = E(nc.sbuf_tensor("rsg2u", [128, 8], F32))
    ssqh = E(nc.sbuf_tensor("ssqh", [128, 8], F32))
    mxh = E(nc.sbuf_tensor("mxh", [128, 8], F32))
    qsh = E(nc.sbuf_tensor("qsh", [128, 8], F32))
    rsd2 = E(nc.sbuf_tensor("rsd2", [128, 8], F32))
    lnst = E(nc.sbuf_tensor("lnst", [128, 12], F32))
    st1 = E(nc.sbuf_tensor("st1", [128, 16], F32))
    st2 = E(nc.sbuf_tensor("st2", [128, 16], F32))
    st3 = E(nc.sbuf_tensor("st3", [128, 16], F32))
    mini1 = E(nc.sbuf_tensor("mini1", [1, 8], F32))
    wfp = [E(nc.sbuf_tensor(f"wfp{i}", [128, 1024], F32)) for i in range(3)]
    sgn = [E(nc.sbuf_tensor(f"sgn{i}", [128, 1024], BF16)) for i in range(4)]
    kg = E(nc.sbuf_tensor("kg", [128, 2048], F32))
    kg2 = E(nc.sbuf_tensor("kg2", [128, 2048], F32))
    minips = E(nc.psum_tensor("minips", [128, 512], F32))

    ktb = [sgn[0], sgn[1]]
    wabs_dump = sgn[3]
    rzb = sgn[2]
    lnz = wfp[0]

    nc.const_aps.register(F32, EPS, consts[:, 0:1])
    nc.const_aps.register(F32, -1.0, consts[:, 1:2])
    nc.const_aps.register(F32, CR, consts[:, 2:3])
    nc.const_aps.register(F32, -CR, consts[:, 3:4])

    es_t = ExitStack()
    gmb = es_t.enter_context(nc.sbuf_tensor("gmb", [128, DM], BF16))
    btb = es_t.enter_context(nc.sbuf_tensor("btb", [128, DM], BF16))
    xf = [es_t.enter_context(nc.sbuf_tensor(f"xf{i}", [128, DM], F32)) for i in range(2)]
    xnT = es_t.enter_context(nc.sbuf_tensor("xnT", [128, DM], BF16))
    kfm = es_t.enter_context(nc.sbuf_tensor("kfm", [128, DM], BF16))
    xnb = [es_t.enter_context(nc.sbuf_tensor(f"xnb{i}", [128, DM], BF16))
           for i in range(2)]
    es_wk = ExitStack()
    wk_sb = es_wk.enter_context(nc.sbuf_tensor("wk_sb", [128, 16 * DM], BF16,
                                               side="right"))
    es_wv = ExitStack()
    wv_sb = es_wv.enter_context(nc.sbuf_tensor("wv_sb", [128, 16 * DM], BF16,
                                               side="right"))
    es_p12 = ExitStack()
    tp_ps = [es_p12.enter_context(nc.psum_tensor(f"tp_ps{i}", [128, 1024], BF16))
             for i in range(3)]
    kv_ps = [es_p12.enter_context(nc.psum_tensor(f"kv_ps{i}", [128, 512], F32))
             for i in range(4)]

    for apv in [(consts[:, 0:1], EPS), (consts[:, 1:2], -1.0),
                (consts[:, 2:3], CR), (consts[:, 3:4], -CR), (ones_bf[:], 1.0),
                (ones1f[:], 1.0), (onesrf[:], 1.0), (wabs_g[:], 0.0),
                (wabs_d[:], 0.0), (ssqh[:], 0.0), (mxh[:], 0.0)]:
        V(lambda e, a=apv[0], vv=apv[1]: e.memset(a, vv), drain=False)
    ck_memset = cV.n
    b.dma("sync", ident[:], identin, s_misc)
    b.dma("sync", gmb[:], gbg, s_misc)
    b.dma("sync", btb[:], gbb, s_misc)
    ck_misc0 = s_misc.n
    b.wait("tensor", (s_misc, ck_misc0), (cV, ck_memset))
    b.wait("scalar", (s_misc, ck_misc0), (cV, ck_memset))
    b.wait("vector", (s_misc, ck_misc0))
    for fc in range(16):
        b.dma("gpsimd", wk_sb[:, fc * DM:(fc + 1) * DM],
              wkt[fc * 128:(fc + 1) * 128, :], s_w)
    for fc in range(16):
        b.dma("gpsimd", wv_sb[:, fc * DM:(fc + 1) * DM],
              wvt[fc * 128:(fc + 1) * 128, :], s_w)
    ck_winit = s_w.n

    abs_chunks = []
    for j in range(176):
        abs_chunks.append((gwt_r[:, j * 1024:(j + 1) * 1024], 0, j))
    for j in range(88):
        abs_chunks.append((dwt_r[:, j * 1024:(j + 1) * 1024], 1, j))
    abs_i = [0]
    abs_sc_ck = [0, 0, 0]

    def emit_abs(n):
        for _ in range(n):
            i = abs_i[0]
            if i >= len(abs_chunks):
                return
            abs_i[0] = i + 1
            src, which, col = abs_chunks[i]
            slot = i % 3
            b.dma("gpsimd", wfp[slot][:], src, s_abs[slot],
                  waits=[(cS, abs_sc_ck[slot])])
            tgt = wabs_g if which == 0 else wabs_d
            abs_sc_ck[slot] = SC(
                lambda e, sl=slot, t=tgt, cc=col: e.activation(
                    wabs_dump[:], wfp[sl][:], ACT.Abs,
                    accum_out=t[:, cc:cc + 1]), waits=[s_abs[slot]])

    st = {
        "xf_free": [(0, 0), (0, 0)],
        "kfm_free": [],
        "xnT_free": 0,
        "xnb_free": [[], []],
        "xn_ck": {},
        "tp_evac": [("cV", 0)] * 3,
        "tp_i": [0],
        "ktb_free": [0, 0],
        "kv_evac": [("cV", 0)] * 4,
    }

    def ln_tile(xbuf, want_f32, xslot, xn_dst=None):
        sck = SC(lambda e, xs=xslot: e.activation(xnb[xs][:], xbuf[:], ACT.Square,
                                        accum_out=lnst[:, 1:2]),
                 waits=[s_x[xslot]] + list(st["xnb_free"][xslot]))
        V(lambda e: e.reduce_sum(lnst[:, 0:1], xbuf[:], axis=AX.X),
          waits=[s_x[xslot]])
        V(lambda e: e.tensor_single_scalar(lnst[:, 2:3], lnst[:, 0:1],
                                           1.0 / DM, ALU.mult))
        V(lambda e: e.tensor_single_scalar(lnst[:, 3:4], lnst[:, 1:2],
                                           1.0 / DM, ALU.mult), waits=[(cS, sck)])
        V(lambda e: e.tensor_tensor(lnst[:, 4:5], lnst[:, 2:3], lnst[:, 2:3],
                                    ALU.mult))
        vck = V(lambda e: e.tensor_tensor(lnst[:, 5:6], lnst[:, 3:4], lnst[:, 4:5],
                                          ALU.subtract))
        sck2 = SC(lambda e: e.activation(lnst[:, 6:7], lnst[:, 5:6], ACT.Sqrt,
                                         bias=EPS), waits=[(cV, vck)])
        V(lambda e: e.reciprocal(lnst[:, 7:8], lnst[:, 6:7]), waits=[(cS, sck2)])
        wfree = list(st["xnb_free"][xslot])
        if want_f32:
            vck = V(lambda e, d=xn_dst: e.tensor_scalar(
                d[:], xbuf[:], lnst[:, 2:3], lnst[:, 7:8], ALU.subtract, ALU.mult))
            vck = V(lambda e, d=xn_dst, xs=xslot: e.tensor_copy(xnb[xs][:], d[:]),
                    waits=wfree)
        else:
            vck = V(lambda e, xs=xslot: e.tensor_scalar(
                xnb[xs][:], xbuf[:], lnst[:, 2:3], lnst[:, 7:8],
                ALU.subtract, ALU.mult), waits=wfree)
        st["xf_free"][xslot] = (vck, cS.n)
        return vck

    def pe_transposes(src_bf, dsts, first_waits):
        for g in range(4):
            sl = st["tp_i"][0] % 3
            st["tp_i"][0] += 1
            bank = tp_ps[sl]
            for j in range(4):
                fc = g * 4 + j
                wl = []
                if j == 0:
                    if g == 0:
                        wl += list(first_waits)
                    en, val = st["tp_evac"][sl]
                    wl.append((CTR[en], val))
                TE(lambda e, bb=bank, f=fc, jj=j, s=src_bf: e.transpose(
                    bb[:, jj * 128:(jj + 1) * 128],
                    s[:, f * 128:(f + 1) * 128], ident[:]),
                   waits=wl, inc=(j == 3))
            tck = cT.n
            if g % 2 == 0:
                v = SC(lambda e, bb=bank, d=dsts[g]: e.copy(d, bb[:, 0:512]),
                       waits=[(cT, tck)])
                st["tp_evac"][sl] = ("cS", v)
            else:
                v = V(lambda e, bb=bank, d=dsts[g]: e.tensor_copy(d, bb[:, 0:512]),
                      waits=[(cT, tck)])
                st["tp_evac"][sl] = ("cV", v)
        return [(cV, cV.n), (cS, cS.n)]

    def proj_mm(w_sb, first_waits):
        cks = []
        for j in range(4):
            for fc in range(16):
                wl = []
                if j == 0 and fc == 0:
                    wl += list(first_waits)
                if fc == 0:
                    en, val = st["kv_evac"][j]
                    wl.append((CTR[en], val))
                TE(lambda e, jj=j, f=fc, w=w_sb: e.matmul(
                    kv_ps[jj][:], xnT[:, f * 128:(f + 1) * 128],
                    w[:, f * DM + jj * 512: f * DM + (jj + 1) * 512],
                    start=(f == 0), stop=(f == 15)),
                   waits=wl, inc=(fc == 15))
            cks.append(cT.n)
        return cks

    def featuremap(dst_bf, q_scale, mm_cks, dst_waits):
        for j in range(4):
            v = V(lambda e, jj=j: e.tensor_tensor(
                kg[:, jj * 512:(jj + 1) * 512], kv_ps[jj][:],
                gmb[:, jj * 512:(jj + 1) * 512], ALU.mult),
                waits=[(cT, mm_cks[j]), (s_misc, ck_misc0)])
            st["kv_evac"][j] = ("cV", v)
        V(lambda e: e.tensor_add(kg[:], kg[:], btb[:]))
        sck = SC(lambda e: e.activation(kg2[:], kg[:], ACT.Square),
                 waits=[(cV, cV.n)])
        V(lambda e: e.reduce_sum(
            st1.ap().rearrange("p (h o) -> p h o", o=1),
            kg.ap().rearrange("p (h d) -> p h d", h=H), axis=AX.X))
        V(lambda e: e.reduce_sum(
            st2.ap().rearrange("p (h o) -> p h o", o=1),
            kg2.ap().rearrange("p (h d) -> p h d", h=H), axis=AX.X),
          waits=[(cS, sck)])
        V(lambda e: e.tensor_single_scalar(st1[:], st1[:], 1.0 / HD, ALU.mult))
        V(lambda e: e.tensor_single_scalar(st2[:], st2[:], 1.0 / HD, ALU.mult))
        V(lambda e: e.tensor_tensor(st3[:], st1[:], st1[:], ALU.mult))
        vck = V(lambda e: e.tensor_tensor(st2[:], st2[:], st3[:], ALU.subtract))
        sck = SC(lambda e: e.activation(st3[:], st2[:], ACT.Sqrt, bias=EPS),
                 waits=[(cV, vck)])
        V(lambda e: e.reciprocal(st3[:], st3[:]), waits=[(cS, sck)])
        if q_scale:
            V(lambda e: e.tensor_single_scalar(st3[:], st3[:], DSC, ALU.mult))
        last = 0
        for h in range(H):
            last = V(lambda e, hh=h, d=dst_bf: e.tensor_scalar(
                d[:, hh * 128:(hh + 1) * 128], kg[:, hh * 128:(hh + 1) * 128],
                st1[:, hh:hh + 1], st3[:, hh:hh + 1], ALU.subtract, ALU.mult),
                waits=(list(dst_waits) if h == 0 else ()), drain=(h == 0))
        return last

    def head_transposes_store(src_bf, src_ck, dst_fn):
        for g in range(4):
            sl = st["tp_i"][0] % 3
            st["tp_i"][0] += 1
            bank = tp_ps[sl]
            for j in range(4):
                h = g * 4 + j
                wl = []
                if j == 0:
                    wl.append((cV, src_ck))
                    en, val = st["tp_evac"][sl]
                    wl.append((CTR[en], val))
                TE(lambda e, bb=bank, hh=h, jj=j, s=src_bf: e.transpose(
                    bb[:, jj * 128:(jj + 1) * 128],
                    s[:, hh * 128:(hh + 1) * 128], ident[:]),
                   waits=wl, inc=(j == 3))
            tck = cT.n
            kslot = g % 2
            if g % 2 == 0:
                v = SC(lambda e, bb=bank, ks=kslot: e.copy(ktb[ks][:, 0:512], bb[:, 0:512]),
                       waits=[(cT, tck), (s_ktb[kslot], st["ktb_free"][kslot])])
                ck = ("cS", v)
            else:
                v = V(lambda e, bb=bank, ks=kslot: e.tensor_copy(ktb[ks][:, 0:512],
                                                                 bb[:, 0:512]),
                      waits=[(cT, tck), (s_ktb[kslot], st["ktb_free"][kslot])])
                ck = ("cV", v)
            st["tp_evac"][sl] = ck
            for j in range(4):
                h = g * 4 + j
                b.dma("sync", dst_fn(h), ktb[kslot][:, j * 128:(j + 1) * 128],
                      s_ktb[kslot], waits=[(CTR[ck[0]], ck[1])])
            st["ktb_free"][kslot] = s_ktb[kslot].n
        return cT.n

    b.dma("sync", xf[0][:], xfull[0:128, :], s_x[0])
    st["xn_ck"][0] = ln_tile(xf[0], want_f32=False, xslot=0)
    for t in range(NT):
        slot = t % 2
        pe_transposes(xnb[slot], [xnT[:, g * 512:(g + 1) * 512] for g in range(4)],
                      first_waits=[(cT, st["xnT_free"]), (cV, st["xn_ck"][t])])
        st["xnb_free"][slot] = [(cT, cT.n)]
        xnT_ready = [(cV, cV.n), (cS, cS.n)]
        kck = proj_mm(wk_sb, first_waits=[(s_w, ck_winit)] + xnT_ready)
        emit_abs(2)
        if t + 1 < NT:
            nslot = (t + 1) % 2
            vf, sf = st["xf_free"][nslot]
            b.dma("sync", xf[nslot][:], xfull[(t + 1) * 128:(t + 2) * 128, :],
                  s_x[nslot], waits=[(cV, vf), (cS, sf)])
        if t + 1 < NT:
            st["xn_ck"][t + 1] = ln_tile(xf[(t + 1) % 2], want_f32=False,
                                         xslot=(t + 1) % 2)
        fm_ck = featuremap(kfm, q_scale=False, mm_cks=kck,
                           dst_waits=st["kfm_free"])
        vck_banks = proj_mm(wv_sb, first_waits=[])
        emit_abs(2)
        st["xnT_free"] = cT.n
        kt_ck = head_transposes_store(kfm, fm_ck,
                                      lambda h, tt=t: ksc[h, :, tt, :])
        st["kfm_free"] = [(cT, kt_ck)]
        for j in range(4):
            v = SC(lambda e, jj=j, xs=slot: e.copy(
                xnb[xs][:, jj * 512:(jj + 1) * 512], kv_ps[jj][:]),
                   waits=[(cT, vck_banks[j])])
            st["kv_evac"][j] = ("cS", v)
        sck = cS.n
        for h in range(H):
            b.dma("sync", vsc[h, :, t, :], xnb[slot][:, h * 128:(h + 1) * 128],
                  s_kvw[slot], waits=[(cS, sck)])
        st["xnb_free"][slot] = [(s_kvw[slot], s_kvw[slot].n)]
        emit_abs(2)

    ck_ph1_mm = cT.n
    b.barrier(bar)

    es_wv.close()
    es_ph2r = ExitStack()
    xn_f = [es_ph2r.enter_context(nc.sbuf_tensor(f"xn_f{i}", [128, DM], F32,
                                                 side="right")) for i in range(2)]
    xq8 = es_ph2r.enter_context(nc.sbuf_tensor("xq8", [128, DM], I8,
                                               side="right"))
    kfm2 = es_ph2r.enter_context(nc.sbuf_tensor("kfm2", [128, DM], BF16,
                                                side="right"))
    gnw_s = es_ph2r.enter_context(nc.sbuf_tensor("gnw_s", [128, DM], F32,
                                                 side="right"))
    b.dma("sync", gnw_s[:], gnwin, s_misc)
    ck_gnw = s_misc.n
    for fc in range(16):
        b.dma("gpsimd", wk_sb[:, fc * DM:(fc + 1) * DM],
              wqt[fc * 128:(fc + 1) * 128, :], s_w, waits=[(cT, ck_ph1_mm)])
    ck_wq = s_w.n
    b.dma("sync", xf[0][:], xown[0:128, :], s_x[0],
          waits=[(cV, st["xf_free"][0][0]), (cS, st["xf_free"][0][1])])
    st["xn_ck"][0] = ln_tile(xf[0], want_f32=True, xslot=0, xn_dst=xn_f[0])
    pend = [None]
    kfm_ring = [kfm, kfm2]
    kfm_free2 = [list(st["kfm_free"]), []]
    for s in range(NOWN):
        slot = s % 2
        pe_transposes(xnb[slot], [xnT[:, g * 512:(g + 1) * 512] for g in range(4)],
                      first_waits=[(cT, st["xnT_free"]), (cV, st["xn_ck"][s])])
        xnT_tr_ck = cT.n
        xnT_ready = [(cV, cV.n), (cS, cS.n)]
        qck = proj_mm(wk_sb, first_waits=[(s_w, ck_wq)] + xnT_ready)
        st["xnT_free"] = qck[-1]
        emit_abs(3)
        if s + 1 < NOWN:
            nslot = (s + 1) % 2
            vf, sf = st["xf_free"][nslot]
            b.dma("sync", xf[nslot][:], xown[(s + 1) * 128:(s + 2) * 128, :],
                  s_x[nslot], waits=[(cV, vf), (cS, sf)])
        if pend[0] is not None:
            pslot, pqb, pfm = pend[0]
            head_transposes_store(kfm_ring[pslot], pfm,
                                  lambda h, ss=s - 1: qsc[h, :, ss, :])
            kfm_free2[pslot] = [(cT, cT.n)]
            head_transposes_store(xnb[pslot], pqb,
                                  lambda fc, ss=s - 1: xqsc[ss, :, fc, :])
            st["xnb_free"][pslot] = [(cT, cT.n)]
        if s + 1 < NOWN:
            st["xn_ck"][s + 1] = ln_tile(xf[(s + 1) % 2], want_f32=True,
                                         xslot=(s + 1) % 2,
                                         xn_dst=xn_f[(s + 1) % 2])
        fm_ck = featuremap(kfm_ring[slot], q_scale=True, mm_cks=qck,
                           dst_waits=kfm_free2[slot])
        emit_abs(3)
        sck = SC(lambda e, xs=slot: e.activation(kg2[:], xn_f[xs][:],
                                                 ACT.Square,
                                                 accum_out=lnst[:, 8:9]))
        vck = V(lambda e: e.tensor_single_scalar(lnst[:, 9:10], lnst[:, 8:9],
                                                 1.0 / DM, ALU.mult),
                waits=[(cS, sck)])
        sck = SC(lambda e: e.activation(lnst[:, 10:11], lnst[:, 9:10], ACT.Sqrt,
                                        bias=EPS), waits=[(cV, vck)])
        V(lambda e: e.reciprocal(lnst[:, 11:12], lnst[:, 10:11]), waits=[(cS, sck)])
        V(lambda e, xs=slot: e.scalar_tensor_tensor(
            kg[:], xn_f[xs][:], lnst[:, 11:12], gnw_s[:], ALU.mult, ALU.mult),
          waits=[(s_misc, ck_gnw)])
        V(lambda e: e.reduce_max(lnst[:, 8:9], kg[:], axis=AX.X,
                                 apply_absolute_value=True))
        V(lambda e: e.tensor_single_scalar(lnst[:, 8:9], lnst[:, 8:9], EPS,
                                           ALU.max))
        V(lambda e, ss=s: e.tensor_single_scalar(rs_g[:, ss:ss + 1], lnst[:, 8:9],
                                                 1.0 / 127.0, ALU.mult))
        V(lambda e: e.reciprocal(lnst[:, 9:10], lnst[:, 8:9]))
        V(lambda e: e.tensor_single_scalar(lnst[:, 9:10], lnst[:, 9:10], 127.0,
                                           ALU.mult))
        V(lambda e: e.tensor_scalar(kg2[:], kg[:], lnst[:, 9:10], CR,
                                    ALU.mult, ALU.add))
        V(lambda e: e.tensor_single_scalar(xq8[:], kg2[:], CR, ALU.subtract))
        qb_ck = V(lambda e, xs=slot: e.tensor_copy(xnb[xs][:], xq8[:]),
                  waits=[(cT, xnT_tr_ck)])
        pend[0] = (slot, qb_ck, fm_ck)
        emit_abs(3)
    pslot, pqb, pfm = pend[0]
    head_transposes_store(kfm_ring[pslot], pfm,
                          lambda h: qsc[h, :, NOWN - 1, :])
    head_transposes_store(xnb[pslot], pqb,
                          lambda fc: xqsc[NOWN - 1, :, fc, :])
    st["xnb_free"][pslot] = [(cT, cT.n)]

    b.wait("sync", s_ktb[0], s_ktb[1], s_kvw[0], s_kvw[1])
    b.barrier(bar)

    assert abs_i[0] == len(abs_chunks), abs_i[0]

    def emit_thresh(src, n_elems, th_p, th_m, hm, ncols):
        ngr = ncols // 11
        V(lambda e, s_=src, ng=ngr: e.reduce_sum(
            wabs_r.ap()[:, 0:ng].rearrange("p (g o) -> p g o", o=1),
            s_.ap().rearrange("p (g d) -> p g d", g=ng), axis=AX.X),
          waits=[(cS, cS.n)])
        V(lambda e, ng=ngr: e.reduce_sum(wabs_s[:, 0:1], wabs_r[:, 0:ng], axis=AX.X))
        TE(lambda e: e.matmul(minips[0:1, 0:1], wabs_s[:], onesrf[:],
                              start=True, stop=True), waits=[(cV, cV.n)])
        tck = cT.n
        V(lambda e: e.tensor_copy(mini1[0:1, 0:1], minips[0:1, 0:1]),
          waits=[(cT, tck)])
        V(lambda e, n=n_elems: e.tensor_single_scalar(
            mini1[0:1, 1:2], mini1[0:1, 0:1], 1.0 / n, ALU.mult))
        V(lambda e: e.tensor_single_scalar(mini1[0:1, 1:2], mini1[0:1, 1:2],
                                           EPS, ALU.max))
        vck = V(lambda e: e.tensor_single_scalar(mini1[0:1, 2:3], mini1[0:1, 1:2],
                                                 0.5, ALU.mult))
        TE(lambda e: e.matmul(minips[:, 0:2], ones1f[:], mini1[0:1, 1:3],
                              start=True, stop=True), waits=[(cV, vck)])
        tck = cT.n
        V(lambda e, hh=hm: e.tensor_single_scalar(hh[:], minips[:, 0:1], 0.5,
                                                  ALU.mult), waits=[(cT, tck)])
        V(lambda e, tp=th_p: e.tensor_copy(tp[:], minips[:, 1:2]))
        vck = V(lambda e, tm=th_m: e.tensor_single_scalar(
            tm[:], minips[:, 1:2], -1.0, ALU.mult))
        return vck
    emit_thresh(wabs_g, N_GW, thg_p, thg_m, hmg, 176)
    emit_thresh(wabs_d, N_DW, thd_p, thd_m, hmd, 88)
    V(lambda e: e.tensor_single_scalar(rsg2[:], rs_g[:], hmg[:, 0:1], ALU.mult))
    V(lambda e: e.tensor_single_scalar(rsg2u[:], rsg2[:], 2.0, ALU.mult))

    b.barrier(bar)

    es_ph2r.close()
    es_wk.close()
    es_t.close()
    es_p12.close()
    es_ar = ExitStack()
    attn_sb = es_ar.enter_context(nc.sbuf_tensor("attn_sb", [128, NOWN * DM],
                                                 BF16, side="right"))
    es_ph3a = ExitStack()
    O_sb = es_ph3a.enter_context(nc.sbuf_tensor("O_sb", [128, H * NOWN * 128],
                                                BF16, side="right"))
    es_ph3 = ExitStack()
    msk_sb = es_ph3.enter_context(nc.sbuf_tensor("msk_sb", [128, 16 * 512], BF16))
    ksb = [es_ph3.enter_context(nc.sbuf_tensor(f"ksb{i}", [128, NC_B * 128], BF16))
           for i in range(2)]
    vsb = [es_ph3.enter_context(nc.sbuf_tensor(f"vsb{i}", [128, NC_B * 128], BF16))
           for i in range(2)]
    qsb = [es_ph3.enter_context(nc.sbuf_tensor(f"qsb{i}", [128, 512], BF16))
           for i in range(2)]
    s2t = [es_ph3.enter_context(nc.sbuf_tensor(f"s2t{i}", [128, 512], BF16))
           for i in range(3)]
    es_pp3 = ExitStack()
    sps = [es_pp3.enter_context(nc.psum_tensor(f"sps{i}", [128, 512], F32))
           for i in range(3)]
    ops_ = [es_pp3.enter_context(nc.psum_tensor(f"ops{i}", [128, 512], F32))
            for i in range(2)]
    zps = [es_pp3.enter_context(nc.psum_tensor(f"zps{i}", [128, 512], F32))
           for i in range(2)]

    kv_slot_free = [0, 0]
    oz_free = [0, 0]
    sps_free = [("cV", 0)] * 3
    rz_free = 0
    msk_free = 0

    for C in range(2):
        NC = NC_A if C == 0 else NC_B
        b.dma("sync", msk_sb.ap().rearrange("k (v q) -> k v q", v=16),
              masksin[C * 16:(C + 1) * 16].rearrange("v k q -> k v q"),
              s_msk, waits=[(cV, msk_free)])
        ck_msk = s_msk.n
        for h in range(H):
            slot = h % 2
            b.dma("sync", ksb[slot][:, 0:NC * 128], ksc[h, :, 0:NC, :],
                  s_kv[slot], waits=[(cT, kv_slot_free[slot])])
            b.dma("sync", vsb[slot][:, 0:NC * 128], vsc[h, :, 0:NC, :],
                  s_kv[slot])
            b.dma("sync", qsb[slot][:], qsc[h, :, 4 * C:4 * C + 4, :],
                  s_q3[slot], waits=[(cT, kv_slot_free[slot])])
            stage_ck = s_kv[slot].n
            stage_q = s_q3[slot].n

            def emit_S(v):
                sb = v % 3
                wl = [(s_kv[slot], stage_ck), (s_q3[slot], stage_q)]
                en, val = sps_free[sb]
                wl.append((CTR[en], val))
                if v == 0:
                    wl.append((cV, oz_free[h % 2]))
                TE(lambda e, sl=slot, vv=v, bb=sb: e.matmul(
                    sps[bb][:], ksb[sl][:, vv * 128:(vv + 1) * 128],
                    qsb[sl][:], start=True, stop=True), waits=wl, inc=True)
                return cT.n

            s_cks = {0: emit_S(0)}
            if NC > 1:
                s_cks[1] = emit_S(1)
            for v in range(NC):
                sb = v % 3
                mm_ck = s_cks[v]
                masked = (C == 0) or (v >= 16)
                if masked:
                    sck = SC(lambda e, bb=sb: e.activation(
                        s2t[bb][:], sps[bb][:], ACT.Square), waits=[(cT, mm_ck)])
                    vck = V(lambda e, bb=sb, vv=v: e.tensor_tensor(
                        s2t[bb][:], s2t[bb][:],
                        msk_sb[:, (vv % 16) * 512:((vv % 16) + 1) * 512],
                        ALU.mult),
                        waits=[(cS, sck), (s_msk, ck_msk)])
                    sps_free[sb] = ("cV", vck)
                    msk_free = vck
                    oz_wl = [(cV, vck)]
                else:
                    sck = SC(lambda e, bb=sb: e.activation(
                        s2t[bb][:], sps[bb][:], ACT.Square), waits=[(cT, mm_ck)])
                    sps_free[sb] = ("cS", sck)
                    oz_wl = [(cS, sck)]
                if v + 2 < NC:
                    s_cks[v + 2] = emit_S(v + 2)
                TE(lambda e, sl=slot, vv=v, bb=sb, hh=h, n=NC: e.matmul(
                    ops_[hh % 2][:], vsb[sl][:, vv * 128:(vv + 1) * 128],
                    s2t[bb][:], start=(vv == 0), stop=(vv == n - 1)),
                   waits=oz_wl, inc=(v == NC - 1))
                TE(lambda e, vv=v, bb=sb, hh=h, n=NC: e.matmul(
                    zps[hh % 2][:], ones_bf[:], s2t[bb][:],
                    start=(vv == 0), stop=(vv == n - 1)),
                   inc=(v == NC - 1))
            kv_slot_free[slot] = cT.n
            zstop = cT.n
            sck = SC(lambda e, hh=h: e.activation(lnz[:, 0:512], zps[hh % 2][:], ACT.Ln,
                                                  bias=EPS),
                     waits=[(cT, zstop), (cV, rz_free)])
            sck = SC(lambda e: e.activation(rzb[:, 0:512], lnz[:, 0:512], ACT.Exp, scale=-1.0))
            vck = V(lambda e, hh=h, cc=C: e.tensor_tensor(
                O_sb[:, hh * 1024 + cc * 512: hh * 1024 + (cc + 1) * 512],
                ops_[hh % 2][:], rzb[:, 0:512], ALU.mult),
                waits=[(cS, sck), (cT, zstop)])
            oz_free[h % 2] = vck
            rz_free = vck

    b.barrier(bar)

    es_ph3.close()
    es_pp3.close()
    es_wfpt = ExitStack()
    wfpt = [es_wfpt.enter_context(nc.sbuf_tensor(f"wfpt{i}", [128, 2048], F32))
            for i in range(2)]
    es_t0 = ExitStack()
    terng = [es_t0.enter_context(nc.sbuf_tensor("terng0", [128, 16 * 512],
                                                BF16))]
    ternu = [es_t0.enter_context(nc.sbuf_tensor("ternu0", [128, 16 * 512],
                                                BF16))]

    gw_ck = [{"cS": cS.n, "cV": cV.n} for _ in range(2)]
    sgn_free = [0, 0]
    tern_free = [0, 0]
    tern_i = [0]

    def tern_ld(src_r, ci):
        i = tern_i[0]
        tern_i[0] = i + 1
        slot = (i // 2) % 2
        half = i % 2
        if half == 0:
            wl = [(CTR[en], v) for en, v in gw_ck[slot].items()]
            b.dma("gpsimd", wfpt[slot][:], src_r[:, ci * 1024:(ci + 2) * 1024],
                  s_gw[slot], waits=wl)
            gw_ck[slot] = {}
        return slot, wfpt[slot][:, half * 1024:(half + 1) * 1024]

    def tern_sc(dst, src_r, ci, th_p, th_m, extra=()):
        slot, wv = tern_ld(src_r, ci)
        SC(lambda e, w=wv: e.activation(
            sgn[0][:], w, ACT.Sign, bias=th_m[:, 0:1]),
            waits=[s_gw[slot], (cV, sgn_free[0])] + list(extra))
        sck = SC(lambda e, w=wv: e.activation(
            sgn[1][:], w, ACT.Sign, bias=th_p[:, 0:1]))
        gw_ck[slot]["cS"] = sck
        vck = V(lambda e, d=dst: e.tensor_add(d, sgn[0][:], sgn[1][:]),
                waits=[(cS, sck)])
        sgn_free[0] = vck
        return vck

    def tern_dve(dst, src_r, ci, th_p, th_m, extra=()):
        slot, wv = tern_ld(src_r, ci)
        V(lambda e, w=wv: e.tensor_single_scalar(
            sgn[2][:], w, th_p[:, 0:1], ALU.is_ge),
          waits=[s_gw[slot]] + list(extra))
        vck = V(lambda e, w=wv: e.tensor_single_scalar(
            sgn[3][:], w, th_m[:, 0:1], ALU.is_le))
        gw_ck[slot]["cV"] = vck
        vck = V(lambda e, d=dst: e.tensor_sub(d, sgn[2][:], sgn[3][:]))
        return vck

    tern_q = []
    for g in range(NPAIR):
        ring = g % 2
        for f2 in range(8):
            tern_q.append(("sc", g, ring, f2))
            tern_q.append(("dve", g, ring, f2))
    tq_i = [0]
    tern_ck_of = {}

    def emit_tern(n, max_g=99):
        for _ in range(n):
            if tq_i[0] >= len(tern_q):
                return
            kind, g, ring, f2 = tern_q[tq_i[0]]
            if g > max_g:
                return
            tq_i[0] += 1
            extra = [(cT, tern_free[ring])] if (kind == "sc" and f2 == 0) or \
                    (kind == "dve" and f2 == 0) else ()
            ci = (g * 8 + f2) * 2
            if kind == "sc":
                tern_sc(terng[ring][:, f2 * 1024:(f2 + 1) * 1024],
                        gwt_r, ci, thg_p, thg_m, extra=extra)
            else:
                ck = tern_dve(ternu[ring][:, f2 * 1024:(f2 + 1) * 1024],
                              gwt_r, ci + 1, thg_p, thg_m, extra=extra)
                if f2 == 7:
                    tern_ck_of[g] = (cV.n, cS.n)

    es_wo = ExitStack()
    wosb = [es_wo.enter_context(nc.sbuf_tensor(f"wosb{i}", [128, 16 * 512], BF16))
            for i in range(2)]
    es_pwo = ExitStack()
    wops = [es_pwo.enter_context(nc.psum_tensor(f"wops{i}", [128, 512], F32))
            for i in range(2)]
    wo_slot_free = [0, 0]
    wo_ps_free = [("cV", 0), ("cV", 0)]
    for m in range(4):
        slot = m % 2
        b.dma("gpsimd", wosb[slot][:], wot_r[:, m * 8192:(m + 1) * 8192],
              s_wo[slot], waits=[(cT, wo_slot_free[slot])])
        wck = s_wo[slot].n
        for s in range(NOWN):
            pb = s % 2
            for hc in range(16):
                wl = []
                if hc == 0:
                    wl = [(s_wo[slot], wck)]
                    en, val = wo_ps_free[pb]
                    wl.append((CTR[en], val))
                TE(lambda e, hh=hc, ss=s, sl=slot, bb=pb: e.matmul(
                    wops[bb][:],
                    O_sb[:, hh * 1024 + ss * 128: hh * 1024 + (ss + 1) * 128],
                    wosb[sl][:, hh * 512:(hh + 1) * 512],
                    start=(hh == 0), stop=(hh == 15)),
                   waits=wl, inc=(hc == 15))
            mck = cT.n
            if s % 2 == 0:
                v = SC(lambda e, ss=s, mm=m, bb=pb: e.copy(
                    attn_sb[:, ss * DM + mm * 512: ss * DM + (mm + 1) * 512],
                    wops[bb][:]), waits=[(cT, mck)])
                wo_ps_free[pb] = ("cS", v)
            else:
                v = V(lambda e, ss=s, mm=m, bb=pb: e.tensor_copy(
                    attn_sb[:, ss * DM + mm * 512: ss * DM + (mm + 1) * 512],
                    wops[bb][:]), waits=[(cT, mck)])
                wo_ps_free[pb] = ("cV", v)
            emit_tern(1, max_g=0)
        wo_slot_free[slot] = cT.n
    ck_wo_v, ck_wo_s = cV.n, cS.n

    b.barrier(bar)

    es_wo.close()
    es_pwo.close()
    es_ph3a.close()
    es_ph4 = ExitStack()
    terng.append(es_ph4.enter_context(nc.sbuf_tensor("terng1", [128, 16 * 512],
                                                     BF16)))
    ternu.append(es_ph4.enter_context(nc.sbuf_tensor("ternu1", [128, 16 * 512],
                                                     BF16)))
    dnw_s = es_ph4.enter_context(nc.sbuf_tensor("dnw_s", [128, INTER], F32))
    xqs = [es_ph4.enter_context(nc.sbuf_tensor(f"xqs{i}", [128, DM], BF16))
           for i in range(3)]
    hgb = [es_ph4.enter_context(nc.sbuf_tensor(f"hgb{i}", [128, 512], F32))
           for i in range(2)]
    es_pg = ExitStack()
    gps = [es_pg.enter_context(nc.psum_tensor(f"gps{i}", [128, 512], F32))
           for i in range(3)]
    ups = [es_pg.enter_context(nc.psum_tensor(f"ups{i}", [128, 512], F32))
           for i in range(3)]
    b.dma("sync", dnw_s[:], dnwin, s_misc)
    ck_dnw = s_misc.n

    xqs_free = [0, 0, 0]
    g_ps_free = [("cV", 0)] * 3
    u_ps_free = [("cV", 0)] * 3
    acc_free = [0, 0]
    hgb_free = [0, 0]

    emit_tern(16, max_g=0)
    for g in range(NPAIR):
        ring = g % 2
        vck_t, sck_t = tern_ck_of[g]
        tern_ck = vck_t
        for s in range(NOWN):
            pb = (g * NOWN + s) % 3
            it = g * NOWN + s
            xslot = it % 3
            emit_tern(2, max_g=g + 1)
            if it == 0:
                for ip in (0, 1):
                    b.dma("sync", xqs[ip % 3][:], xqsc[ip % NOWN], s_xq[ip % 3],
                          waits=[(cT, xqs_free[ip % 3])])
            ipre = it + 2
            if ipre < NPAIR * NOWN:
                b.dma("sync", xqs[ipre % 3][:], xqsc[ipre % NOWN],
                      s_xq[ipre % 3], waits=[(cT, xqs_free[ipre % 3])])
            xq_ck = s_xq[xslot].n
            for fc in range(16):
                wl = []
                if fc == 0:
                    wl = [(s_xq[xslot], xq_ck)]
                    if s == 0:
                        wl += [(cV, tern_ck), (cS, sck_t)]
                    if it < 3:
                        wl += [(cV, ck_wo_v), (cS, ck_wo_s)]
                    en, val = g_ps_free[pb]
                    wl.append((CTR[en], val))
                TE(lambda e, f=fc, r=ring, bb=pb, xs=xslot: e.matmul(
                    gps[bb][:], xqs[xs][:, f * 128:(f + 1) * 128],
                    terng[r][:, f * 512:(f + 1) * 512],
                    start=(f == 0), stop=(f == 15)), waits=wl, inc=(fc == 15))
            gck = cT.n
            for fc in range(16):
                wl = []
                if fc == 0:
                    if it < 3:
                        wl += [(cV, ck_wo_v), (cS, ck_wo_s)]
                    en, val = u_ps_free[pb]
                    wl.append((CTR[en], val))
                TE(lambda e, f=fc, r=ring, bb=pb, xs=xslot: e.matmul(
                    ups[bb][:], xqs[xs][:, f * 128:(f + 1) * 128],
                    ternu[r][:, f * 512:(f + 1) * 512],
                    start=(f == 0), stop=(f == 15)), waits=wl, inc=(fc == 15))
            uck = cT.n
            xqs_free[xslot] = uck
            if s == NOWN - 1:
                tern_free[ring] = uck
            sck = SC(lambda e, ss=s, bb=pb: e.activation(
                kg[:, 0:512], gps[bb][:], ACT.Silu, scale=rsg2[:, ss:ss + 1]),
                waits=[(cT, gck)])
            g_ps_free[pb] = ("cS", sck)
            vck = V(lambda e, ss=s, bb=pb: e.scalar_tensor_tensor(
                kg2[:, 0:512], ups[bb][:], rsg2u[:, ss:ss + 1], kg[:, 0:512],
                ALU.mult, ALU.mult), waits=[(cS, sck), (cT, uck)])
            u_ps_free[pb] = ("cV", vck)
            aslot = s % 2
            sck = SC(lambda e, a=aslot: e.activation(
                kg2[:, 512:1024], kg2[:, 0:512], ACT.Square,
                accum_out=acc_tmp[a][:]),
                waits=[(cV, vck), (cV, acc_free[aslot])])
            V(lambda e, ss=s, a=aslot: e.tensor_add(
                ssqh[:, ss:ss + 1], ssqh[:, ss:ss + 1], acc_tmp[a][:]),
              waits=[(cS, sck)])
            acc_free[aslot] = cV.n
            hslot = s % 2
            V(lambda e, gg=g, hs=hslot: e.tensor_tensor(
                hgb[hs][:], kg2[:, 0:512],
                dnw_s[:, gg * 512:(gg + 1) * 512], ALU.mult),
              waits=[(s_hgw[hslot], hgb_free[hslot]), (s_misc, ck_dnw)])
            V(lambda e, hs=hslot: e.reduce_max(lnst[:, 8:9], hgb[hs][:], axis=AX.X,
                                               apply_absolute_value=True))
            vck = V(lambda e, ss=s: e.tensor_tensor(
                mxh[:, ss:ss + 1], mxh[:, ss:ss + 1], lnst[:, 8:9], ALU.max))
            b.dma("sync", hgscr[s][:, g * 512:(g + 1) * 512], hgb[hslot][:],
                  s_hgw[hslot], waits=[(cV, vck)])
            hgb_free[hslot] = s_hgw[hslot].n

    V(lambda e: e.tensor_single_scalar(st1[:, 0:8], ssqh[:], 1.0 / INTER,
                                       ALU.mult))
    sck = SC(lambda e: e.activation(st1[:, 0:8], st1[:, 0:8], ACT.Sqrt, bias=EPS),
             waits=[(cV, cV.n)])
    V(lambda e: e.reciprocal(st1[:, 0:8], st1[:, 0:8]), waits=[(cS, sck)])
    V(lambda e: e.tensor_tensor(st2[:, 0:8], mxh[:], st1[:, 0:8], ALU.mult))
    V(lambda e: e.tensor_single_scalar(st2[:, 0:8], st2[:, 0:8], EPS, ALU.max))
    V(lambda e: e.tensor_single_scalar(rsd2[:], st2[:, 0:8], 1.0 / 127.0,
                                       ALU.mult))
    V(lambda e: e.tensor_single_scalar(rsd2[:], rsd2[:], hmd[:, 0:1], ALU.mult))
    V(lambda e: e.reciprocal(st3[:, 0:8], st2[:, 0:8]))
    V(lambda e: e.tensor_single_scalar(st3[:, 0:8], st3[:, 0:8], 127.0, ALU.mult))
    V(lambda e: e.tensor_tensor(qsh[:], st1[:, 0:8], st3[:, 0:8], ALU.mult))
    ck_qsh = cV.n

    b.wait("sync", s_hgw[0], s_hgw[1])
    b.barrier(bar)

    es_pg.close()
    es_ph4.close()
    es_t0.close()
    es_dt = ExitStack()
    dtern = [es_dt.enter_context(nc.sbuf_tensor(f"dtern{i}", [128, NFI * 512],
                                                BF16)) for i in range(2)]
    dtern_free = [0, 0]
    dtq = []
    for m in range(4):
        for f2 in range(NFI // 2):
            dtq.append((m, m % 2, f2))
    dtq_i = [0]
    dtern_ck_of = {}

    def emit_dtern(n, max_m=99):
        for _ in range(n):
            if dtq_i[0] >= len(dtq):
                return
            m, ring, f2 = dtq[dtq_i[0]]
            if m > max_m:
                return
            dtq_i[0] += 1
            extra = [(cT, dtern_free[ring])] if f2 == 0 else ()
            ck = tern_sc(dtern[ring][:, f2 * 1024:(f2 + 1) * 1024],
                         dwt_r, m * 22 + f2, thd_p, thd_m, extra=extra)
            if f2 == NFI // 2 - 1:
                dtern_ck_of[m] = ck

    es_4b = ExitStack()
    hgr = [es_4b.enter_context(nc.sbuf_tensor(f"hgr{i}", [128, 512], F32))
           for i in range(3)]
    q8b = es_4b.enter_context(nc.sbuf_tensor("q8b", [128, 512], I8))
    q8b2 = es_4b.enter_context(nc.sbuf_tensor("q8b2", [128, 512], I8))
    hgb4 = [es_4b.enter_context(nc.sbuf_tensor(f"hgb4_{i}", [128, 512], BF16))
            for i in range(3)]
    hqTb = [es_4b.enter_context(nc.sbuf_tensor(f"hqTb{i}", [128, INTER], BF16))
            for i in range(2)]
    es_p4b = ExitStack()
    tp4 = [es_p4b.enter_context(nc.psum_tensor(f"tp4_{i}", [128, 1024], BF16))
           for i in range(2)]
    hgr_free = [("cV", 0)] * 3
    hqTb_free = [0, 0]
    hgb_free = [("cT", 0)] * 3
    tp4_free = [("cV", 0)] * 4
    for s in range(NOWN):
        hb = s % 2
        b.wait("vector", (s_hqw[hb], hqTb_free[hb]))
        b.wait("scalar", (s_hqw[hb], hqTb_free[hb]))
        for jc in range(NPAIR):
            rs = (s * NPAIR + jc) % 3
            hs4 = (s * NPAIR + jc) % 3
            tp4s = jc % 4
            b.dma("sync", hgr[rs][:], hgscr[s][:, jc * 512:(jc + 1) * 512],
                  s_hgr[rs], waits=[(CTR[hgr_free[rs][0]], hgr_free[rs][1])])
            hen, hval = hgb_free[hs4]
            if jc % 2 == 0:
                vck1 = V(lambda e, r=rs, ss=s: e.tensor_scalar(
                    kg[:, 0:512], hgr[r][:], qsh[:, ss:ss + 1], CR, ALU.mult,
                    ALU.add), waits=[s_hgr[rs]])
                hgr_free[rs] = ("cV", vck1)
                V(lambda e: e.tensor_single_scalar(q8b[:], kg[:, 0:512],
                                                   CR, ALU.subtract))
                vck = V(lambda e, h4=hs4: e.tensor_copy(hgb4[h4][:], q8b[:]),
                        waits=[(CTR[hen], hval)])
                ev = ("cV", vck)
            else:
                sck1 = SC(lambda e, r=rs, ss=s: e.activation(
                    kg[:, 512:1024], hgr[r][:], ACT.Identity,
                    scale=qsh[:, ss:ss + 1], bias=CR),
                    waits=[s_hgr[rs]] + ([(cV, ck_qsh)]
                                         if (s == 0 and jc == 1) else []))
                hgr_free[rs] = ("cS", sck1)
                SC(lambda e: e.activation(q8b2[:], kg[:, 512:1024], ACT.Identity,
                                          bias=-CR))
                vck = SC(lambda e, h4=hs4: e.copy(hgb4[h4][:], q8b2[:]),
                         waits=[(CTR[hen], hval)])
                ev = ("cS", vck)
            hsrc = hgb4[hs4]
            tb = tp4s % 2
            th = tp4s // 2
            for j in range(4):
                wl = []
                if j == 0:
                    wl = [(CTR[ev[0]], ev[1])]
                    en, val = tp4_free[tp4s]
                    wl.append((CTR[en], val))
                TE(lambda e, jj=j, bb=tb, hh=th, hs_=hsrc: e.transpose(
                    tp4[bb][:, hh * 512 + jj * 128:hh * 512 + (jj + 1) * 128],
                    hs_[:, jj * 128:(jj + 1) * 128],
                    ident[:]), waits=wl, inc=(j == 3))
            tck = cT.n
            hgb_free[hs4] = ("cT", tck)
            wl2 = [(cT, tck)]
            if jc % 2 == 0:
                v = V(lambda e, j_=jc, hb_=hb, bb=tb, hh=th: e.tensor_copy(
                    hqTb[hb_][:, j_ * 512:(j_ + 1) * 512],
                    tp4[bb][:, hh * 512:(hh + 1) * 512]), waits=wl2)
                tp4_free[tp4s] = ("cV", v)
            else:
                v = SC(lambda e, j_=jc, hb_=hb, bb=tb, hh=th: e.copy(
                    hqTb[hb_][:, j_ * 512:(j_ + 1) * 512],
                    tp4[bb][:, hh * 512:(hh + 1) * 512]), waits=wl2)
                tp4_free[tp4s] = ("cS", v)
            if jc % 4 == 0:
                emit_dtern(1, max_m=0)
        b.dma("gpsimd", hqtd[s], hqTb[hb][:], s_hqw[hb],
              waits=[(cV, cV.n), (cS, cS.n)])
        hqTb_free[hb] = s_hqw[hb].n

    b.wait("sync", s_hqw[0], s_hqw[1])
    b.barrier(bar)

    es_p4b.close()
    es_4b.close()
    es_4c = ExitStack()
    hqts = [es_4c.enter_context(nc.sbuf_tensor(f"hqts{i}", [128, INTER], BF16))
            for i in range(2)]
    xob = [es_4c.enter_context(nc.sbuf_tensor(f"xob{i}", [128, 512], F32))
           for i in range(2)]
    outb = [es_4c.enter_context(nc.sbuf_tensor(f"outb{i}", [128, 512], F32))
            for i in range(2)]
    es_p4c = ExitStack()
    dps = [es_p4c.enter_context(nc.psum_tensor(f"dps{i}", [128, 512], F32))
           for i in range(2)]
    hqts_free = [0, 0]
    d_ps_free = [("cV", 0), ("cV", 0)]
    xob_free = [0, 0]
    outb_free = [0, 0]
    it_list = [(m, s) for m in range(4) for s in range(NOWN)]
    hq_ck = [0, 0]
    b.dma("sync", hqts[0][:], hqtd[0], s_hqt[0], waits=[(cT, hqts_free[0])])
    hq_ck[0] = s_hqt[0].n
    for it_i, (m, s) in enumerate(it_list):
        ring = m % 2
        last_tern = dtern_ck_of[m]
        if True:
            hs = s % 2
            pb = s % 2
            emit_dtern(3, max_m=m + 1)
            if it_i + 1 < len(it_list):
                ns = it_list[it_i + 1][1]
                nhs = ns % 2
                b.dma("sync", hqts[nhs][:], hqtd[ns], s_hqt[nhs],
                      waits=[(cT, hqts_free[nhs])])
                hq_ck[nhs] = s_hqt[nhs].n
            b.dma("sync", xob[hs][:],
                  xown[s * 128:(s + 1) * 128, m * 512:(m + 1) * 512],
                  s_xo[hs], waits=[(cV, xob_free[hs])])
            hck = hq_ck[hs]
            xck = s_xo[hs].n
            for fc in range(NFI):
                wl = []
                if fc == 0:
                    wl = [(s_hqt[hs], hck)]
                    if s == 0:
                        wl.append((cV, last_tern))
                    en, val = d_ps_free[pb]
                    wl.append((CTR[en], val))
                TE(lambda e, f=fc, h_=hs, bb=pb, r=ring: e.matmul(
                    dps[bb][:], hqts[h_][:, f * 128:(f + 1) * 128],
                    dtern[r][:, f * 512:(f + 1) * 512],
                    start=(f == 0), stop=(f == NFI - 1)), waits=wl,
                   inc=(fc == NFI - 1))
            hqts_free[hs] = cT.n
            if s == NOWN - 1:
                dtern_free[ring] = cT.n
            mck = cT.n
            ob = s % 2
            vck = V(lambda e, ss=s, mm=m, bb=pb, o=ob: e.scalar_tensor_tensor(
                outb[o][:], dps[bb][:], rsd2[:, ss:ss + 1],
                attn_sb[:, ss * DM + mm * 512: ss * DM + (mm + 1) * 512],
                ALU.mult, ALU.add),
                waits=[(cT, mck), (s_out[ob], outb_free[ob])])
            d_ps_free[pb] = ("cV", cV.n)
            vck = V(lambda e, o=ob, h_=hs: e.tensor_add(
                outb[o][:], outb[o][:], xob[h_][:]), waits=[(s_xo[hs], xck)])
            xob_free[hs] = vck
            b.dma("sync", outd[s * 128:(s + 1) * 128, m * 512:(m + 1) * 512],
                  outb[ob][:], s_out[ob], waits=[(cV, vck)])
            outb_free[ob] = s_out[ob].n

    b.wait("sync", (s_out[0], s_out[0].n), (s_out[1], s_out[1].n))

    with nc.Block() as block:
        for name in ENGINES:
            fns = b.steps[name]
            getattr(block, name)(lambda eng, fns=fns: [f(eng) for f in fns])
    return nc


_NC_CACHE = None


def _chunk_major_gw(gw):
    gwt = np.ascontiguousarray(gw.T)
    g3 = gwt.reshape(8, 2, 128, 22, 512)
    arr = np.empty((128, 11, 8, 2, 2, 512), dtype=np.float32)
    arr[:, :, :, 0] = g3[:, :, :, 0:11].transpose(2, 3, 0, 1, 4)
    arr[:, :, :, 1] = g3[:, :, :, 11:22].transpose(2, 3, 0, 1, 4)
    return np.ascontiguousarray(arr.reshape(128, 176 * 1024))


def _chunk_major_dw(dw):
    dwt = np.ascontiguousarray(dw.T)
    d3 = dwt.reshape(22, 2, 128, 4, 512)
    dr = d3.transpose(2, 3, 0, 1, 4)
    return np.ascontiguousarray(dr).reshape(128, 88 * 1024)


def _host_prepare(inputs):
    x = np.asarray(inputs["L0_attn_hidden_states"], dtype=np.float32)
    Wq = np.asarray(inputs["L0_attn_Wq"], dtype=np.float32)
    Wk = np.asarray(inputs["L0_attn_Wk"], dtype=np.float32)
    Wv = np.asarray(inputs["L0_attn_Wv"], dtype=np.float32)
    Wo = np.asarray(inputs["L0_attn_Wo"], dtype=np.float32)
    gamma = np.asarray(inputs["L0_attn_gamma"], dtype=np.float32)
    beta = np.asarray(inputs["L0_attn_beta"], dtype=np.float32)
    gw = np.asarray(inputs["L0_ffn_gate_w"], dtype=np.float32)
    dw = np.asarray(inputs["L0_ffn_down_w"], dtype=np.float32)
    gnw = np.asarray(inputs["L0_ffn_gate_norm_w"], dtype=np.float32)
    dnw = np.asarray(inputs["L0_ffn_down_norm_w"], dtype=np.float32)

    bf = ml_dtypes.bfloat16
    shared = {
        "wqt": np.ascontiguousarray(Wq.T),
        "wkt": np.ascontiguousarray(Wk.T),
        "wvt": np.ascontiguousarray(Wv.T),
        "wot_r": np.ascontiguousarray(
            Wo.T.astype(np.float32).reshape(16, 128, 4, 512)
            .transpose(1, 2, 0, 3).reshape(128, 4 * 16 * 512)),
        "gwt_r": _chunk_major_gw(gw),
        "dwt_r": _chunk_major_dw(dw),
        "gbg": np.ascontiguousarray(np.broadcast_to(
            np.tile(gamma, H), (128, DM))).astype(bf),
        "gbb": np.ascontiguousarray(np.broadcast_to(
            np.tile(beta, H), (128, DM))).astype(bf),
        "gnwin": np.ascontiguousarray(np.broadcast_to(gnw, (128, DM))),
        "dnwin": np.ascontiguousarray(np.broadcast_to(dnw, (128, INTER))),
        "identin": np.eye(128, dtype=bf),
    }

    in_maps = []
    own_tiles_all = []
    for c in range(8):
        bidx, qq = c // 4, c % 4
        xb = x[bidx]
        own = list(range(4 * qq, 4 * qq + 4)) + list(range(28 - 4 * qq, 32 - 4 * qq))
        own_tiles_all.append(own)
        xo = np.concatenate([xb[tt * 128:(tt + 1) * 128] for tt in own], axis=0)
        masks = np.zeros((NT, 128, 512), dtype=np.float32)
        kidx = np.arange(128)
        qA = np.concatenate([tt * 128 + np.arange(128) for tt in own[:4]])
        qB = np.concatenate([tt * 128 + np.arange(128) for tt in own[4:]])
        for v in range(16):
            kk = v * 128 + kidx
            masks[v] = (qA[None, :] >= kk[:, None]).astype(np.float32)
        for v in range(16, 32):
            kk = v * 128 + kidx
            masks[v] = (qB[None, :] >= kk[:, None]).astype(np.float32)
        im = dict(shared)
        im["xfull"] = np.ascontiguousarray(xb)
        im["xown"] = np.ascontiguousarray(xo)
        im["masksin"] = masks.astype(bf)
        in_maps.append(im)
    return in_maps, own_tiles_all


def kernel(**inputs):
    global _NC_CACHE
    in_maps, own_tiles_all = _host_prepare(inputs)
    if _NC_CACHE is None:
        _NC_CACHE = build_nc()
    nc = _NC_CACHE
    res = run_bass_kernel_spmd(nc, in_maps, core_ids=list(range(8)))
    out = np.empty((2, T, DM), dtype=np.float32)
    for c in range(8):
        bidx = c // 4
        o = np.asarray(res.results[c]["outd"], dtype=np.float32)
        for i, tt in enumerate(own_tiles_all[c]):
            out[bidx, tt * 128:(tt + 1) * 128] = o[i * 128:(i + 1) * 128]
    return out


# version 35
# speedup vs baseline: 1.0345x; 1.0345x over previous
import numpy as np
import ml_dtypes
from contextlib import ExitStack

import concourse.bass as bass
import concourse.mybir as mybir
from concourse.bass_utils import run_bass_kernel_spmd

F32 = mybir.dt.float32
BF16 = mybir.dt.bfloat16
I8 = mybir.dt.int8
AX = mybir.AxisListType
ALU = mybir.AluOpType
ACT = mybir.ActivationFunctionType

T = 4096
DM = 2048
H = 16
HD = 128
INTER = 5632
G2 = 2 * INTER
NT = 32
NOWN = 8
EPS = 1e-5
CR = 12582912.0
DSC = float(HD) ** -0.5
NC_A, NC_B = 16, 32
NFI = INTER // 128
NPAIR = INTER // 512
N_GW = DM * G2
N_DW = INTER * DM

ENGINES = ("sync", "gpsimd", "tensor", "vector", "scalar")


class Sem:
    __slots__ = ("h", "n")
    def __init__(self, h):
        self.h = h
        self.n = 0


class B:
    def __init__(self, nc, es):
        self.steps = {e: [] for e in ENGINES}
        self.nc = nc
        self._es = es

    def sem(self, name):
        return Sem(self._es.enter_context(self.nc.semaphore(name)))

    def _wl(self, waits):
        out = []
        for w in waits:
            if isinstance(w, tuple):
                s, v = w
                out.append((s.h, s.n if v is None else v))
            else:
                out.append((w.h, w.n))
        return out

    def op(self, engine, fn, waits=(), inc=None, inc_n=1, drain=False):
        wl = self._wl(waits)
        if inc is not None:
            inc.n += inc_n
        ih = inc.h if inc is not None else None
        def f(eng, wl=wl, fn=fn, ih=ih, inc_n=inc_n, drain=drain):
            for h, v in wl:
                if v > 0:
                    eng.wait_ge(h, v)
            if drain:
                eng.drain()
            ins = fn(eng)
            if ih is not None:
                ins.then_inc(ih, inc_n)
        self.steps[engine].append(f)
        return inc.n if inc is not None else None

    def dma(self, engine, out, in_, sem, waits=()):
        return self.op(engine, lambda e, o=out, i=in_: e.dma_start(out=o, in_=i),
                       waits=waits, inc=sem, inc_n=16)

    def wait(self, engine, *waits):
        wl = self._wl(waits)
        def f(eng, wl=wl):
            for h, v in wl:
                if v > 0:
                    eng.wait_ge(h, v)
        self.steps[engine].append(f)

    def barrier(self, bar):
        target = bar.n + len(ENGINES)
        bar.n = target
        for e in ENGINES:
            def f(eng, h=bar.h, t=target):
                eng.sem_inc(h, 1)
                eng.wait_ge(h, t)
            self.steps[e].append(f)


def build_nc():
    nc = bass.Bass("TRN2", target_bir_lowering=False, debug=False)

    xfull = nc.dram_tensor("xfull", [T, DM], F32, kind="ExternalInput").ap()
    xown = nc.dram_tensor("xown", [NOWN * 128, DM], F32, kind="ExternalInput").ap()
    wqt = nc.dram_tensor("wqt", [DM, DM], F32, kind="ExternalInput").ap()
    wkt = nc.dram_tensor("wkt", [DM, DM], F32, kind="ExternalInput").ap()
    wvt = nc.dram_tensor("wvt", [DM, DM], F32, kind="ExternalInput").ap()
    wot_r = nc.dram_tensor("wot_r", [128, 4 * 16 * 512], F32,
                           kind="ExternalInput").ap()
    gwt_r = nc.dram_tensor("gwt_r", [128, 176 * 1024], F32,
                           kind="ExternalInput").ap()
    dwt_r = nc.dram_tensor("dwt_r", [128, 88 * 1024], F32,
                           kind="ExternalInput").ap()
    gbg = nc.dram_tensor("gbg", [128, DM], BF16, kind="ExternalInput").ap()
    gbb = nc.dram_tensor("gbb", [128, DM], BF16, kind="ExternalInput").ap()
    gnwin = nc.dram_tensor("gnwin", [128, DM], F32, kind="ExternalInput").ap()
    dnwin = nc.dram_tensor("dnwin", [128, INTER], F32, kind="ExternalInput").ap()
    identin = nc.dram_tensor("identin", [128, 128], BF16, kind="ExternalInput").ap()
    masksin = nc.dram_tensor("masksin", [NT, 128, 512], BF16, kind="ExternalInput").ap()
    outd = nc.dram_tensor("outd", [NOWN * 128, DM], F32, kind="ExternalOutput").ap()
    ksc = nc.dram_tensor("ksc", [H, 128, NT, 128], BF16, kind="Internal").ap()
    vsc = nc.dram_tensor("vsc", [H, 128, NT, 128], BF16, kind="Internal").ap()
    qsc = nc.dram_tensor("qsc", [H, 128, NOWN, 128], BF16, kind="Internal").ap()
    xqsc = nc.dram_tensor("xqsc", [NOWN, 128, 16, 128], BF16, kind="Internal").ap()
    hgscr = nc.dram_tensor("hgscr", [NOWN, 128, INTER], F32, kind="Internal").ap()
    hqtd = nc.dram_tensor("hqtd", [NOWN, 128, NFI, 128], BF16, kind="Internal").ap()

    es = ExitStack()
    E = es.enter_context
    b = B(nc, es)

    bar = b.sem("bar")
    s_x = [b.sem("s_x0"), b.sem("s_x1")]
    s_w = b.sem("s_w")
    s_abs = [b.sem(f"s_abs{i}") for i in range(3)]
    s_kvw = [b.sem("s_kvw0"), b.sem("s_kvw1")]
    s_ktb = [b.sem("s_ktb0"), b.sem("s_ktb1")]
    s_kv = [b.sem("s_kv0"), b.sem("s_kv1")]
    s_q3 = [b.sem("s_q30"), b.sem("s_q31")]
    s_msk = b.sem("s_msk")
    s_wo = [b.sem("s_wo0"), b.sem("s_wo1")]
    s_gw = [b.sem(f"s_gw{i}") for i in range(3)]
    s_xq = [b.sem(f"s_xq{i}") for i in range(3)]
    s_hgw = [b.sem("s_hgw0"), b.sem("s_hgw1")]
    s_hgr = [b.sem(f"s_hgr{i}") for i in range(4)]
    s_hqw = [b.sem("s_hqw0"), b.sem("s_hqw1")]
    s_hqt = [b.sem("s_hqt0"), b.sem("s_hqt1")]
    s_xo = [b.sem("s_xo0"), b.sem("s_xo1")]
    s_out = [b.sem("s_out0"), b.sem("s_out1")]
    s_misc = b.sem("s_misc")
    cV = b.sem("cV")
    cS = b.sem("cS")
    cT = b.sem("cT")
    CTR = {"cV": cV, "cS": cS, "cT": cT}

    def V(fn, waits=(), drain=True):
        w = list(waits) + ([(cV, cV.n)] if drain else [])
        return b.op("vector", fn, waits=w, inc=cV)

    def SC(fn, waits=(), drain=True):
        w = list(waits) + ([(cS, cS.n)] if drain else [])
        return b.op("scalar", fn, waits=w, inc=cS)

    def TE(fn, waits=(), inc=True):
        return b.op("tensor", fn, waits=waits, inc=(cT if inc else None))

    consts = E(nc.sbuf_tensor("consts", [128, 8], F32))
    ones_bf = E(nc.sbuf_tensor("ones_bf", [128, 128], BF16))
    ones1f = E(nc.sbuf_tensor("ones1f", [1, 128], F32))
    onesrf = E(nc.sbuf_tensor("onesrf", [128, 1], F32))
    ident = E(nc.sbuf_tensor("ident", [128, 128], BF16))
    wabs_g = E(nc.sbuf_tensor("wabs_g", [128, 176], F32))
    wabs_d = E(nc.sbuf_tensor("wabs_d", [128, 88], F32))
    wabs_r = E(nc.sbuf_tensor("wabs_r", [128, 32], F32))
    wabs_s = E(nc.sbuf_tensor("wabs_s", [128, 1], F32))
    acc_tmp = [E(nc.sbuf_tensor(f"acc_tmp{i}", [128, 1], F32)) for i in range(2)]
    thg_p = E(nc.sbuf_tensor("thg_p", [128, 1], F32))
    thg_m = E(nc.sbuf_tensor("thg_m", [128, 1], F32))
    thd_p = E(nc.sbuf_tensor("thd_p", [128, 1], F32))
    thd_m = E(nc.sbuf_tensor("thd_m", [128, 1], F32))
    hmg = E(nc.sbuf_tensor("hmg", [128, 1], F32))
    hmd = E(nc.sbuf_tensor("hmd", [128, 1], F32))
    rs_g = E(nc.sbuf_tensor("rs_g", [128, 8], F32))
    rsg2 = E(nc.sbuf_tensor("rsg2", [128, 8], F32))
    rsg2u = E(nc.sbuf_tensor("rsg2u", [128, 8], F32))
    ssqh = E(nc.sbuf_tensor("ssqh", [128, 8], F32))
    mxh = E(nc.sbuf_tensor("mxh", [128, 8], F32))
    qsh = E(nc.sbuf_tensor("qsh", [128, 8], F32))
    rsd2 = E(nc.sbuf_tensor("rsd2", [128, 8], F32))
    lnst = E(nc.sbuf_tensor("lnst", [128, 12], F32))
    st1 = E(nc.sbuf_tensor("st1", [128, 16], F32))
    st2 = E(nc.sbuf_tensor("st2", [128, 16], F32))
    st3 = E(nc.sbuf_tensor("st3", [128, 16], F32))
    mini1 = E(nc.sbuf_tensor("mini1", [1, 8], F32))
    wfp = [E(nc.sbuf_tensor(f"wfp{i}", [128, 1024], F32)) for i in range(3)]
    sgn = [E(nc.sbuf_tensor(f"sgn{i}", [128, 1024], BF16)) for i in range(4)]
    kg = E(nc.sbuf_tensor("kg", [128, 2048], F32))
    kg2 = E(nc.sbuf_tensor("kg2", [128, 2048], F32))
    minips = E(nc.psum_tensor("minips", [128, 512], F32))

    ktb = [sgn[0], sgn[1]]
    wabs_dump = sgn[3]
    rzb = sgn[2]
    lnz = wfp[0]

    nc.const_aps.register(F32, EPS, consts[:, 0:1])
    nc.const_aps.register(F32, -1.0, consts[:, 1:2])
    nc.const_aps.register(F32, CR, consts[:, 2:3])
    nc.const_aps.register(F32, -CR, consts[:, 3:4])

    es_t = ExitStack()
    gmb = es_t.enter_context(nc.sbuf_tensor("gmb", [128, DM], BF16))
    btb = es_t.enter_context(nc.sbuf_tensor("btb", [128, DM], BF16))
    xf = [es_t.enter_context(nc.sbuf_tensor(f"xf{i}", [128, DM], F32)) for i in range(2)]
    xnT = es_t.enter_context(nc.sbuf_tensor("xnT", [128, DM], BF16))
    kfm = es_t.enter_context(nc.sbuf_tensor("kfm", [128, DM], BF16))
    xnb = [es_t.enter_context(nc.sbuf_tensor(f"xnb{i}", [128, DM], BF16))
           for i in range(2)]
    es_wk = ExitStack()
    wk_sb = es_wk.enter_context(nc.sbuf_tensor("wk_sb", [128, 16 * DM], BF16,
                                               side="right"))
    es_wv = ExitStack()
    wv_sb = es_wv.enter_context(nc.sbuf_tensor("wv_sb", [128, 16 * DM], BF16,
                                               side="right"))
    es_p12 = ExitStack()
    tp_ps = [es_p12.enter_context(nc.psum_tensor(f"tp_ps{i}", [128, 1024], BF16))
             for i in range(3)]
    kv_ps = [es_p12.enter_context(nc.psum_tensor(f"kv_ps{i}", [128, 512], F32))
             for i in range(4)]

    for apv in [(consts[:, 0:1], EPS), (consts[:, 1:2], -1.0),
                (consts[:, 2:3], CR), (consts[:, 3:4], -CR), (ones_bf[:], 1.0),
                (ones1f[:], 1.0), (onesrf[:], 1.0), (wabs_g[:], 0.0),
                (wabs_d[:], 0.0), (ssqh[:], 0.0), (mxh[:], 0.0)]:
        V(lambda e, a=apv[0], vv=apv[1]: e.memset(a, vv), drain=False)
    ck_memset = cV.n
    b.dma("sync", ident[:], identin, s_misc)
    b.dma("sync", gmb[:], gbg, s_misc)
    b.dma("sync", btb[:], gbb, s_misc)
    ck_misc0 = s_misc.n
    b.wait("tensor", (s_misc, ck_misc0), (cV, ck_memset))
    b.wait("scalar", (s_misc, ck_misc0), (cV, ck_memset))
    b.wait("vector", (s_misc, ck_misc0))
    for fc in range(16):
        b.dma("gpsimd", wk_sb[:, fc * DM:(fc + 1) * DM],
              wkt[fc * 128:(fc + 1) * 128, :], s_w)
    for fc in range(16):
        b.dma("gpsimd", wv_sb[:, fc * DM:(fc + 1) * DM],
              wvt[fc * 128:(fc + 1) * 128, :], s_w)
    ck_winit = s_w.n

    abs_chunks = []
    for j in range(176):
        abs_chunks.append((gwt_r[:, j * 1024:(j + 1) * 1024], 0, j))
    for j in range(88):
        abs_chunks.append((dwt_r[:, j * 1024:(j + 1) * 1024], 1, j))
    abs_i = [0]
    abs_sc_ck = [0, 0, 0]

    def emit_abs(n):
        for _ in range(n):
            i = abs_i[0]
            if i >= len(abs_chunks):
                return
            abs_i[0] = i + 1
            src, which, col = abs_chunks[i]
            slot = i % 3
            b.dma("gpsimd", wfp[slot][:], src, s_abs[slot],
                  waits=[(cS, abs_sc_ck[slot])])
            tgt = wabs_g if which == 0 else wabs_d
            abs_sc_ck[slot] = SC(
                lambda e, sl=slot, t=tgt, cc=col: e.activation(
                    wabs_dump[:], wfp[sl][:], ACT.Abs,
                    accum_out=t[:, cc:cc + 1]), waits=[s_abs[slot]])

    st = {
        "xf_free": [(0, 0), (0, 0)],
        "kfm_free": [],
        "xnT_free": 0,
        "xnb_free": [[], []],
        "xn_ck": {},
        "tp_evac": [("cV", 0)] * 3,
        "tp_i": [0],
        "ktb_free": [0, 0],
        "kv_evac": [("cV", 0)] * 4,
    }

    def ln_tile(xbuf, want_f32, xslot, xn_dst=None):
        sck = SC(lambda e, xs=xslot: e.activation(xnb[xs][:], xbuf[:], ACT.Square,
                                        accum_out=lnst[:, 1:2]),
                 waits=[s_x[xslot]] + list(st["xnb_free"][xslot]))
        V(lambda e: e.reduce_sum(lnst[:, 0:1], xbuf[:], axis=AX.X),
          waits=[s_x[xslot]])
        V(lambda e: e.tensor_single_scalar(lnst[:, 2:3], lnst[:, 0:1],
                                           1.0 / DM, ALU.mult))
        V(lambda e: e.tensor_single_scalar(lnst[:, 3:4], lnst[:, 1:2],
                                           1.0 / DM, ALU.mult), waits=[(cS, sck)])
        V(lambda e: e.tensor_tensor(lnst[:, 4:5], lnst[:, 2:3], lnst[:, 2:3],
                                    ALU.mult))
        vck = V(lambda e: e.tensor_tensor(lnst[:, 5:6], lnst[:, 3:4], lnst[:, 4:5],
                                          ALU.subtract))
        sck2 = SC(lambda e: e.activation(lnst[:, 6:7], lnst[:, 5:6], ACT.Sqrt,
                                         bias=EPS), waits=[(cV, vck)])
        V(lambda e: e.reciprocal(lnst[:, 7:8], lnst[:, 6:7]), waits=[(cS, sck2)])
        wfree = list(st["xnb_free"][xslot])
        if want_f32:
            vck = V(lambda e, d=xn_dst: e.tensor_scalar(
                d[:], xbuf[:], lnst[:, 2:3], lnst[:, 7:8], ALU.subtract, ALU.mult))
            vck = V(lambda e, d=xn_dst, xs=xslot: e.tensor_copy(xnb[xs][:], d[:]),
                    waits=wfree)
        else:
            vck = V(lambda e, xs=xslot: e.tensor_scalar(
                xnb[xs][:], xbuf[:], lnst[:, 2:3], lnst[:, 7:8],
                ALU.subtract, ALU.mult), waits=wfree)
        st["xf_free"][xslot] = (vck, cS.n)
        return vck

    def pe_transposes(src_bf, dsts, first_waits):
        for g in range(4):
            sl = st["tp_i"][0] % 3
            st["tp_i"][0] += 1
            bank = tp_ps[sl]
            for j in range(4):
                fc = g * 4 + j
                wl = []
                if j == 0:
                    if g == 0:
                        wl += list(first_waits)
                    en, val = st["tp_evac"][sl]
                    wl.append((CTR[en], val))
                TE(lambda e, bb=bank, f=fc, jj=j, s=src_bf: e.transpose(
                    bb[:, jj * 128:(jj + 1) * 128],
                    s[:, f * 128:(f + 1) * 128], ident[:]),
                   waits=wl, inc=(j == 3))
            tck = cT.n
            if g % 2 == 0:
                v = SC(lambda e, bb=bank, d=dsts[g]: e.copy(d, bb[:, 0:512]),
                       waits=[(cT, tck)])
                st["tp_evac"][sl] = ("cS", v)
            else:
                v = V(lambda e, bb=bank, d=dsts[g]: e.tensor_copy(d, bb[:, 0:512]),
                      waits=[(cT, tck)])
                st["tp_evac"][sl] = ("cV", v)
        return [(cV, cV.n), (cS, cS.n)]

    def proj_mm(w_sb, first_waits):
        cks = []
        for j in range(4):
            for fc in range(16):
                wl = []
                if j == 0 and fc == 0:
                    wl += list(first_waits)
                if fc == 0:
                    en, val = st["kv_evac"][j]
                    wl.append((CTR[en], val))
                TE(lambda e, jj=j, f=fc, w=w_sb: e.matmul(
                    kv_ps[jj][:], xnT[:, f * 128:(f + 1) * 128],
                    w[:, f * DM + jj * 512: f * DM + (jj + 1) * 512],
                    start=(f == 0), stop=(f == 15)),
                   waits=wl, inc=(fc == 15))
            cks.append(cT.n)
        return cks

    def featuremap_evac(mm_cks):
        for j in range(4):
            v = V(lambda e, jj=j: e.tensor_tensor(
                kg[:, jj * 512:(jj + 1) * 512], kv_ps[jj][:],
                gmb[:, jj * 512:(jj + 1) * 512], ALU.mult),
                waits=[(cT, mm_cks[j]), (s_misc, ck_misc0)])
            st["kv_evac"][j] = ("cV", v)

    def featuremap(dst_bf, q_scale, dst_waits):
        V(lambda e: e.tensor_add(kg[:], kg[:], btb[:]))
        sck = SC(lambda e: e.activation(kg2[:], kg[:], ACT.Square),
                 waits=[(cV, cV.n)])
        V(lambda e: e.reduce_sum(
            st1.ap().rearrange("p (h o) -> p h o", o=1),
            kg.ap().rearrange("p (h d) -> p h d", h=H), axis=AX.X))
        V(lambda e: e.reduce_sum(
            st2.ap().rearrange("p (h o) -> p h o", o=1),
            kg2.ap().rearrange("p (h d) -> p h d", h=H), axis=AX.X),
          waits=[(cS, sck)])
        V(lambda e: e.tensor_single_scalar(st1[:], st1[:], 1.0 / HD, ALU.mult))
        V(lambda e: e.tensor_single_scalar(st2[:], st2[:], 1.0 / HD, ALU.mult))
        V(lambda e: e.tensor_tensor(st3[:], st1[:], st1[:], ALU.mult))
        vck = V(lambda e: e.tensor_tensor(st2[:], st2[:], st3[:], ALU.subtract))
        sck = SC(lambda e: e.activation(st3[:], st2[:], ACT.Sqrt, bias=EPS),
                 waits=[(cV, vck)])
        V(lambda e: e.reciprocal(st3[:], st3[:]), waits=[(cS, sck)])
        if q_scale:
            V(lambda e: e.tensor_single_scalar(st3[:], st3[:], DSC, ALU.mult))
        last = 0
        for h in range(H):
            last = V(lambda e, hh=h, d=dst_bf: e.tensor_scalar(
                d[:, hh * 128:(hh + 1) * 128], kg[:, hh * 128:(hh + 1) * 128],
                st1[:, hh:hh + 1], st3[:, hh:hh + 1], ALU.subtract, ALU.mult),
                waits=(list(dst_waits) if h == 0 else ()), drain=(h == 0))
        return last

    def head_transposes_store(src_bf, src_ck, dst_fn):
        for g in range(4):
            sl = st["tp_i"][0] % 3
            st["tp_i"][0] += 1
            bank = tp_ps[sl]
            for j in range(4):
                h = g * 4 + j
                wl = []
                if j == 0:
                    wl.append((cV, src_ck))
                    en, val = st["tp_evac"][sl]
                    wl.append((CTR[en], val))
                TE(lambda e, bb=bank, hh=h, jj=j, s=src_bf: e.transpose(
                    bb[:, jj * 128:(jj + 1) * 128],
                    s[:, hh * 128:(hh + 1) * 128], ident[:]),
                   waits=wl, inc=(j == 3))
            tck = cT.n
            kslot = g % 2
            if g % 2 == 0:
                v = SC(lambda e, bb=bank, ks=kslot: e.copy(ktb[ks][:, 0:512], bb[:, 0:512]),
                       waits=[(cT, tck), (s_ktb[kslot], st["ktb_free"][kslot])])
                ck = ("cS", v)
            else:
                v = V(lambda e, bb=bank, ks=kslot: e.tensor_copy(ktb[ks][:, 0:512],
                                                                 bb[:, 0:512]),
                      waits=[(cT, tck), (s_ktb[kslot], st["ktb_free"][kslot])])
                ck = ("cV", v)
            st["tp_evac"][sl] = ck
            for j in range(4):
                h = g * 4 + j
                b.dma("sync", dst_fn(h), ktb[kslot][:, j * 128:(j + 1) * 128],
                      s_ktb[kslot], waits=[(CTR[ck[0]], ck[1])])
            st["ktb_free"][kslot] = s_ktb[kslot].n
        return cT.n

    b.dma("sync", xf[0][:], xfull[0:128, :], s_x[0])
    st["xn_ck"][0] = ln_tile(xf[0], want_f32=False, xslot=0)
    for t in range(NT):
        slot = t % 2
        pe_transposes(xnb[slot], [xnT[:, g * 512:(g + 1) * 512] for g in range(4)],
                      first_waits=[(cT, st["xnT_free"]), (cV, st["xn_ck"][t])])
        st["xnb_free"][slot] = [(cT, cT.n)]
        xnT_ready = [(cV, cV.n), (cS, cS.n)]
        kck = proj_mm(wk_sb, first_waits=[(s_w, ck_winit)] + xnT_ready)
        emit_abs(2)
        if t + 1 < NT:
            nslot = (t + 1) % 2
            vf, sf = st["xf_free"][nslot]
            b.dma("sync", xf[nslot][:], xfull[(t + 1) * 128:(t + 2) * 128, :],
                  s_x[nslot], waits=[(cV, vf), (cS, sf)])
        featuremap_evac(kck)
        vck_banks = proj_mm(wv_sb, first_waits=[])
        st["xnT_free"] = cT.n
        if t + 1 < NT:
            st["xn_ck"][t + 1] = ln_tile(xf[(t + 1) % 2], want_f32=False,
                                         xslot=(t + 1) % 2)
        fm_ck = featuremap(kfm, q_scale=False,
                           dst_waits=st["kfm_free"])
        emit_abs(2)
        kt_ck = head_transposes_store(kfm, fm_ck,
                                      lambda h, tt=t: ksc[h, :, tt, :])
        st["kfm_free"] = [(cT, kt_ck)]
        for j in range(4):
            v = SC(lambda e, jj=j, xs=slot: e.copy(
                xnb[xs][:, jj * 512:(jj + 1) * 512], kv_ps[jj][:]),
                   waits=[(cT, vck_banks[j])])
            st["kv_evac"][j] = ("cS", v)
        sck = cS.n
        for h in range(H):
            b.dma("sync", vsc[h, :, t, :], xnb[slot][:, h * 128:(h + 1) * 128],
                  s_kvw[slot], waits=[(cS, sck)])
        st["xnb_free"][slot] = [(s_kvw[slot], s_kvw[slot].n)]
        emit_abs(2)

    ck_ph1_mm = cT.n
    b.barrier(bar)

    es_wv.close()
    es_ph2r = ExitStack()
    xn_f = [es_ph2r.enter_context(nc.sbuf_tensor(f"xn_f{i}", [128, DM], F32,
                                                 side="right")) for i in range(2)]
    xq8 = es_ph2r.enter_context(nc.sbuf_tensor("xq8", [128, DM], I8,
                                               side="right"))
    kfm2 = es_ph2r.enter_context(nc.sbuf_tensor("kfm2", [128, DM], BF16,
                                                side="right"))
    gnw_s = es_ph2r.enter_context(nc.sbuf_tensor("gnw_s", [128, DM], F32,
                                                 side="right"))
    b.dma("sync", gnw_s[:], gnwin, s_misc)
    ck_gnw = s_misc.n
    for fc in range(16):
        b.dma("gpsimd", wk_sb[:, fc * DM:(fc + 1) * DM],
              wqt[fc * 128:(fc + 1) * 128, :], s_w, waits=[(cT, ck_ph1_mm)])
    ck_wq = s_w.n
    b.dma("sync", xf[0][:], xown[0:128, :], s_x[0],
          waits=[(cV, st["xf_free"][0][0]), (cS, st["xf_free"][0][1])])
    st["xn_ck"][0] = ln_tile(xf[0], want_f32=True, xslot=0, xn_dst=xn_f[0])
    pend = [None]
    kfm_ring = [kfm, kfm2]
    kfm_free2 = [list(st["kfm_free"]), []]
    for s in range(NOWN):
        slot = s % 2
        pe_transposes(xnb[slot], [xnT[:, g * 512:(g + 1) * 512] for g in range(4)],
                      first_waits=[(cT, st["xnT_free"]), (cV, st["xn_ck"][s])])
        xnT_tr_ck = cT.n
        xnT_ready = [(cV, cV.n), (cS, cS.n)]
        qck = proj_mm(wk_sb, first_waits=[(s_w, ck_wq)] + xnT_ready)
        st["xnT_free"] = qck[-1]
        featuremap_evac(qck)
        emit_abs(3)
        if s + 1 < NOWN:
            nslot = (s + 1) % 2
            vf, sf = st["xf_free"][nslot]
            b.dma("sync", xf[nslot][:], xown[(s + 1) * 128:(s + 2) * 128, :],
                  s_x[nslot], waits=[(cV, vf), (cS, sf)])
        if pend[0] is not None:
            pslot, pqb, pfm = pend[0]
            head_transposes_store(kfm_ring[pslot], pfm,
                                  lambda h, ss=s - 1: qsc[h, :, ss, :])
            kfm_free2[pslot] = [(cT, cT.n)]
            head_transposes_store(xnb[pslot], pqb,
                                  lambda fc, ss=s - 1: xqsc[ss, :, fc, :])
            st["xnb_free"][pslot] = [(cT, cT.n)]
        if s + 1 < NOWN:
            st["xn_ck"][s + 1] = ln_tile(xf[(s + 1) % 2], want_f32=True,
                                         xslot=(s + 1) % 2,
                                         xn_dst=xn_f[(s + 1) % 2])
        fm_ck = featuremap(kfm_ring[slot], q_scale=True,
                           dst_waits=kfm_free2[slot])
        emit_abs(3)
        sck = SC(lambda e, xs=slot: e.activation(kg2[:], xn_f[xs][:],
                                                 ACT.Square,
                                                 accum_out=lnst[:, 8:9]))
        vck = V(lambda e: e.tensor_single_scalar(lnst[:, 9:10], lnst[:, 8:9],
                                                 1.0 / DM, ALU.mult),
                waits=[(cS, sck)])
        sck = SC(lambda e: e.activation(lnst[:, 10:11], lnst[:, 9:10], ACT.Sqrt,
                                        bias=EPS), waits=[(cV, vck)])
        V(lambda e: e.reciprocal(lnst[:, 11:12], lnst[:, 10:11]), waits=[(cS, sck)])
        V(lambda e, xs=slot: e.scalar_tensor_tensor(
            kg[:], xn_f[xs][:], lnst[:, 11:12], gnw_s[:], ALU.mult, ALU.mult),
          waits=[(s_misc, ck_gnw)])
        V(lambda e: e.reduce_max(lnst[:, 8:9], kg[:], axis=AX.X,
                                 apply_absolute_value=True))
        V(lambda e: e.tensor_single_scalar(lnst[:, 8:9], lnst[:, 8:9], EPS,
                                           ALU.max))
        V(lambda e, ss=s: e.tensor_single_scalar(rs_g[:, ss:ss + 1], lnst[:, 8:9],
                                                 1.0 / 127.0, ALU.mult))
        V(lambda e: e.reciprocal(lnst[:, 9:10], lnst[:, 8:9]))
        V(lambda e: e.tensor_single_scalar(lnst[:, 9:10], lnst[:, 9:10], 127.0,
                                           ALU.mult))
        V(lambda e: e.tensor_scalar(kg2[:], kg[:], lnst[:, 9:10], CR,
                                    ALU.mult, ALU.add))
        V(lambda e: e.tensor_single_scalar(xq8[:], kg2[:], CR, ALU.subtract))
        qb_ck = V(lambda e, xs=slot: e.tensor_copy(xnb[xs][:], xq8[:]),
                  waits=[(cT, xnT_tr_ck)])
        pend[0] = (slot, qb_ck, fm_ck)
        emit_abs(3)
    pslot, pqb, pfm = pend[0]
    head_transposes_store(kfm_ring[pslot], pfm,
                          lambda h: qsc[h, :, NOWN - 1, :])
    head_transposes_store(xnb[pslot], pqb,
                          lambda fc: xqsc[NOWN - 1, :, fc, :])
    st["xnb_free"][pslot] = [(cT, cT.n)]

    b.wait("sync", s_ktb[0], s_ktb[1], s_kvw[0], s_kvw[1])
    b.barrier(bar)

    assert abs_i[0] == len(abs_chunks), abs_i[0]

    def emit_thresh(src, n_elems, th_p, th_m, hm, ncols):
        ngr = ncols // 11
        V(lambda e, s_=src, ng=ngr: e.reduce_sum(
            wabs_r.ap()[:, 0:ng].rearrange("p (g o) -> p g o", o=1),
            s_.ap().rearrange("p (g d) -> p g d", g=ng), axis=AX.X),
          waits=[(cS, cS.n)])
        V(lambda e, ng=ngr: e.reduce_sum(wabs_s[:, 0:1], wabs_r[:, 0:ng], axis=AX.X))
        TE(lambda e: e.matmul(minips[0:1, 0:1], wabs_s[:], onesrf[:],
                              start=True, stop=True), waits=[(cV, cV.n)])
        tck = cT.n
        V(lambda e: e.tensor_copy(mini1[0:1, 0:1], minips[0:1, 0:1]),
          waits=[(cT, tck)])
        V(lambda e, n=n_elems: e.tensor_single_scalar(
            mini1[0:1, 1:2], mini1[0:1, 0:1], 1.0 / n, ALU.mult))
        V(lambda e: e.tensor_single_scalar(mini1[0:1, 1:2], mini1[0:1, 1:2],
                                           EPS, ALU.max))
        vck = V(lambda e: e.tensor_single_scalar(mini1[0:1, 2:3], mini1[0:1, 1:2],
                                                 0.5, ALU.mult))
        TE(lambda e: e.matmul(minips[:, 0:2], ones1f[:], mini1[0:1, 1:3],
                              start=True, stop=True), waits=[(cV, vck)])
        tck = cT.n
        V(lambda e, hh=hm: e.tensor_single_scalar(hh[:], minips[:, 0:1], 0.5,
                                                  ALU.mult), waits=[(cT, tck)])
        V(lambda e, tp=th_p: e.tensor_copy(tp[:], minips[:, 1:2]))
        vck = V(lambda e, tm=th_m: e.tensor_single_scalar(
            tm[:], minips[:, 1:2], -1.0, ALU.mult))
        return vck
    emit_thresh(wabs_g, N_GW, thg_p, thg_m, hmg, 176)
    emit_thresh(wabs_d, N_DW, thd_p, thd_m, hmd, 88)
    V(lambda e: e.tensor_single_scalar(rsg2[:], rs_g[:], hmg[:, 0:1], ALU.mult))
    V(lambda e: e.tensor_single_scalar(rsg2u[:], rsg2[:], 2.0, ALU.mult))

    b.barrier(bar)

    es_ph2r.close()
    es_wk.close()
    es_t.close()
    es_p12.close()
    es_ar = ExitStack()
    attn_sb = es_ar.enter_context(nc.sbuf_tensor("attn_sb", [128, NOWN * DM],
                                                 BF16, side="right"))
    es_ph3a = ExitStack()
    O_sb = es_ph3a.enter_context(nc.sbuf_tensor("O_sb", [128, H * NOWN * 128],
                                                BF16, side="right"))
    es_ph3 = ExitStack()
    msk_sb = es_ph3.enter_context(nc.sbuf_tensor("msk_sb", [128, 16 * 512], BF16))
    ksb = [es_ph3.enter_context(nc.sbuf_tensor(f"ksb{i}", [128, NC_B * 128], BF16))
           for i in range(2)]
    vsb = [es_ph3.enter_context(nc.sbuf_tensor(f"vsb{i}", [128, NC_B * 128], BF16))
           for i in range(2)]
    qsb = [es_ph3.enter_context(nc.sbuf_tensor(f"qsb{i}", [128, 512], BF16))
           for i in range(2)]
    s2t = [es_ph3.enter_context(nc.sbuf_tensor(f"s2t{i}", [128, 512], BF16))
           for i in range(3)]
    es_pp3 = ExitStack()
    sps = [es_pp3.enter_context(nc.psum_tensor(f"sps{i}", [128, 512], F32))
           for i in range(3)]
    ops_ = [es_pp3.enter_context(nc.psum_tensor(f"ops{i}", [128, 512], F32))
            for i in range(2)]
    zps = [es_pp3.enter_context(nc.psum_tensor(f"zps{i}", [128, 512], F32))
           for i in range(2)]

    kv_slot_free = [0, 0]
    oz_free = [0, 0]
    sps_free = [("cV", 0)] * 3
    rz_free = 0
    msk_free = 0

    for C in range(2):
        NC = NC_A if C == 0 else NC_B
        b.dma("sync", msk_sb.ap().rearrange("k (v q) -> k v q", v=16),
              masksin[C * 16:(C + 1) * 16].rearrange("v k q -> k v q"),
              s_msk, waits=[(cV, msk_free)])
        ck_msk = s_msk.n
        for h in range(H):
            slot = h % 2
            b.dma("sync", ksb[slot][:, 0:NC * 128], ksc[h, :, 0:NC, :],
                  s_kv[slot], waits=[(cT, kv_slot_free[slot])])
            b.dma("sync", vsb[slot][:, 0:NC * 128], vsc[h, :, 0:NC, :],
                  s_kv[slot])
            b.dma("sync", qsb[slot][:], qsc[h, :, 4 * C:4 * C + 4, :],
                  s_q3[slot], waits=[(cT, kv_slot_free[slot])])
            stage_ck = s_kv[slot].n
            stage_q = s_q3[slot].n

            def emit_S(v):
                sb = v % 3
                wl = [(s_kv[slot], stage_ck), (s_q3[slot], stage_q)]
                en, val = sps_free[sb]
                wl.append((CTR[en], val))
                if v == 0:
                    wl.append((cV, oz_free[h % 2]))
                TE(lambda e, sl=slot, vv=v, bb=sb: e.matmul(
                    sps[bb][:], ksb[sl][:, vv * 128:(vv + 1) * 128],
                    qsb[sl][:], start=True, stop=True), waits=wl, inc=True)
                return cT.n

            s_cks = {0: emit_S(0)}
            if NC > 1:
                s_cks[1] = emit_S(1)
            for v in range(NC):
                sb = v % 3
                mm_ck = s_cks[v]
                masked = (C == 0) or (v >= 16)
                if masked:
                    sck = SC(lambda e, bb=sb: e.activation(
                        s2t[bb][:], sps[bb][:], ACT.Square), waits=[(cT, mm_ck)])
                    vck = V(lambda e, bb=sb, vv=v: e.tensor_tensor(
                        s2t[bb][:], s2t[bb][:],
                        msk_sb[:, (vv % 16) * 512:((vv % 16) + 1) * 512],
                        ALU.mult),
                        waits=[(cS, sck), (s_msk, ck_msk)])
                    sps_free[sb] = ("cV", vck)
                    msk_free = vck
                    oz_wl = [(cV, vck)]
                else:
                    sck = SC(lambda e, bb=sb: e.activation(
                        s2t[bb][:], sps[bb][:], ACT.Square), waits=[(cT, mm_ck)])
                    sps_free[sb] = ("cS", sck)
                    oz_wl = [(cS, sck)]
                if v + 2 < NC:
                    s_cks[v + 2] = emit_S(v + 2)
                TE(lambda e, sl=slot, vv=v, bb=sb, hh=h, n=NC: e.matmul(
                    ops_[hh % 2][:], vsb[sl][:, vv * 128:(vv + 1) * 128],
                    s2t[bb][:], start=(vv == 0), stop=(vv == n - 1)),
                   waits=oz_wl, inc=(v == NC - 1))
                TE(lambda e, vv=v, bb=sb, hh=h, n=NC: e.matmul(
                    zps[hh % 2][:], ones_bf[:], s2t[bb][:],
                    start=(vv == 0), stop=(vv == n - 1)),
                   inc=(v == NC - 1))
            kv_slot_free[slot] = cT.n
            zstop = cT.n
            sck = SC(lambda e, hh=h: e.activation(lnz[:, 0:512], zps[hh % 2][:], ACT.Ln,
                                                  bias=EPS),
                     waits=[(cT, zstop), (cV, rz_free)])
            sck = SC(lambda e: e.activation(rzb[:, 0:512], lnz[:, 0:512], ACT.Exp, scale=-1.0))
            vck = V(lambda e, hh=h, cc=C: e.tensor_tensor(
                O_sb[:, hh * 1024 + cc * 512: hh * 1024 + (cc + 1) * 512],
                ops_[hh % 2][:], rzb[:, 0:512], ALU.mult),
                waits=[(cS, sck), (cT, zstop)])
            oz_free[h % 2] = vck
            rz_free = vck

    b.barrier(bar)

    es_ph3.close()
    es_pp3.close()
    es_wfpt = ExitStack()
    wfpt = [es_wfpt.enter_context(nc.sbuf_tensor(f"wfpt{i}", [128, 2048], F32))
            for i in range(2)]
    es_t0 = ExitStack()
    terng = [es_t0.enter_context(nc.sbuf_tensor("terng0", [128, 16 * 512],
                                                BF16))]
    ternu = [es_t0.enter_context(nc.sbuf_tensor("ternu0", [128, 16 * 512],
                                                BF16))]

    gw_ck = [{"cS": cS.n, "cV": cV.n} for _ in range(2)]
    sgn_free = [0, 0]
    tern_free = [0, 0]
    tern_i = [0]

    def tern_ld(src_r, ci):
        i = tern_i[0]
        tern_i[0] = i + 1
        slot = (i // 2) % 2
        half = i % 2
        if half == 0:
            wl = [(CTR[en], v) for en, v in gw_ck[slot].items()]
            b.dma("gpsimd", wfpt[slot][:], src_r[:, ci * 1024:(ci + 2) * 1024],
                  s_gw[slot], waits=wl)
            gw_ck[slot] = {}
        return slot, wfpt[slot][:, half * 1024:(half + 1) * 1024]

    def tern_sc(dst, src_r, ci, th_p, th_m, extra=()):
        slot, wv = tern_ld(src_r, ci)
        SC(lambda e, w=wv: e.activation(
            sgn[0][:], w, ACT.Sign, bias=th_m[:, 0:1]),
            waits=[s_gw[slot], (cV, sgn_free[0])] + list(extra))
        sck = SC(lambda e, w=wv: e.activation(
            sgn[1][:], w, ACT.Sign, bias=th_p[:, 0:1]))
        gw_ck[slot]["cS"] = sck
        vck = V(lambda e, d=dst: e.tensor_add(d, sgn[0][:], sgn[1][:]),
                waits=[(cS, sck)])
        sgn_free[0] = vck
        return vck

    def tern_dve(dst, src_r, ci, th_p, th_m, extra=()):
        slot, wv = tern_ld(src_r, ci)
        V(lambda e, w=wv: e.tensor_single_scalar(
            sgn[2][:], w, th_p[:, 0:1], ALU.is_ge),
          waits=[s_gw[slot]] + list(extra))
        vck = V(lambda e, w=wv: e.tensor_single_scalar(
            sgn[3][:], w, th_m[:, 0:1], ALU.is_le))
        gw_ck[slot]["cV"] = vck
        vck = V(lambda e, d=dst: e.tensor_sub(d, sgn[2][:], sgn[3][:]))
        return vck

    tern_q = []
    for g in range(NPAIR):
        ring = g % 2
        for f2 in range(8):
            tern_q.append(("sc", g, ring, f2))
            tern_q.append(("dve", g, ring, f2))
    tq_i = [0]
    tern_ck_of = {}

    def emit_tern(n, max_g=99):
        for _ in range(n):
            if tq_i[0] >= len(tern_q):
                return
            kind, g, ring, f2 = tern_q[tq_i[0]]
            if g > max_g:
                return
            tq_i[0] += 1
            extra = [(cT, tern_free[ring])] if (kind == "sc" and f2 == 0) or \
                    (kind == "dve" and f2 == 0) else ()
            ci = (g * 8 + f2) * 2
            if kind == "sc":
                tern_sc(terng[ring][:, f2 * 1024:(f2 + 1) * 1024],
                        gwt_r, ci, thg_p, thg_m, extra=extra)
            else:
                ck = tern_dve(ternu[ring][:, f2 * 1024:(f2 + 1) * 1024],
                              gwt_r, ci + 1, thg_p, thg_m, extra=extra)
                if f2 == 7:
                    tern_ck_of[g] = (cV.n, cS.n)

    es_wo = ExitStack()
    wosb = [es_wo.enter_context(nc.sbuf_tensor(f"wosb{i}", [128, 16 * 512], BF16))
            for i in range(2)]
    es_pwo = ExitStack()
    wops = [es_pwo.enter_context(nc.psum_tensor(f"wops{i}", [128, 512], F32))
            for i in range(2)]
    wo_slot_free = [0, 0]
    wo_ps_free = [("cV", 0), ("cV", 0)]
    for m in range(4):
        slot = m % 2
        b.dma("gpsimd", wosb[slot][:], wot_r[:, m * 8192:(m + 1) * 8192],
              s_wo[slot], waits=[(cT, wo_slot_free[slot])])
        wck = s_wo[slot].n
        for s in range(NOWN):
            pb = s % 2
            for hc in range(16):
                wl = []
                if hc == 0:
                    wl = [(s_wo[slot], wck)]
                    en, val = wo_ps_free[pb]
                    wl.append((CTR[en], val))
                TE(lambda e, hh=hc, ss=s, sl=slot, bb=pb: e.matmul(
                    wops[bb][:],
                    O_sb[:, hh * 1024 + ss * 128: hh * 1024 + (ss + 1) * 128],
                    wosb[sl][:, hh * 512:(hh + 1) * 512],
                    start=(hh == 0), stop=(hh == 15)),
                   waits=wl, inc=(hc == 15))
            mck = cT.n
            if s % 2 == 0:
                v = SC(lambda e, ss=s, mm=m, bb=pb: e.copy(
                    attn_sb[:, ss * DM + mm * 512: ss * DM + (mm + 1) * 512],
                    wops[bb][:]), waits=[(cT, mck)])
                wo_ps_free[pb] = ("cS", v)
            else:
                v = V(lambda e, ss=s, mm=m, bb=pb: e.tensor_copy(
                    attn_sb[:, ss * DM + mm * 512: ss * DM + (mm + 1) * 512],
                    wops[bb][:]), waits=[(cT, mck)])
                wo_ps_free[pb] = ("cV", v)
            emit_tern(1, max_g=0)
        wo_slot_free[slot] = cT.n
    ck_wo_v, ck_wo_s = cV.n, cS.n

    b.barrier(bar)

    es_wo.close()
    es_pwo.close()
    es_ph3a.close()
    es_ph4 = ExitStack()
    terng.append(es_ph4.enter_context(nc.sbuf_tensor("terng1", [128, 16 * 512],
                                                     BF16)))
    ternu.append(es_ph4.enter_context(nc.sbuf_tensor("ternu1", [128, 16 * 512],
                                                     BF16)))
    dnw_s = es_ph4.enter_context(nc.sbuf_tensor("dnw_s", [128, INTER], F32))
    xqs = [es_ph4.enter_context(nc.sbuf_tensor(f"xqs{i}", [128, DM], BF16))
           for i in range(3)]
    hgb = [es_ph4.enter_context(nc.sbuf_tensor(f"hgb{i}", [128, 512], F32))
           for i in range(2)]
    es_pg = ExitStack()
    gps = [es_pg.enter_context(nc.psum_tensor(f"gps{i}", [128, 512], F32))
           for i in range(3)]
    ups = [es_pg.enter_context(nc.psum_tensor(f"ups{i}", [128, 512], F32))
           for i in range(3)]
    b.dma("sync", dnw_s[:], dnwin, s_misc)
    ck_dnw = s_misc.n

    xqs_free = [0, 0, 0]
    g_ps_free = [("cV", 0)] * 3
    u_ps_free = [("cV", 0)] * 3
    acc_free = [0, 0]
    hgb_free = [0, 0]

    emit_tern(16, max_g=0)
    for g in range(NPAIR):
        ring = g % 2
        vck_t, sck_t = tern_ck_of[g]
        tern_ck = vck_t
        for s in range(NOWN):
            pb = (g * NOWN + s) % 3
            it = g * NOWN + s
            xslot = it % 3
            emit_tern(2, max_g=g + 1)
            if it == 0:
                for ip in (0, 1):
                    b.dma("sync", xqs[ip % 3][:], xqsc[ip % NOWN], s_xq[ip % 3],
                          waits=[(cT, xqs_free[ip % 3])])
            ipre = it + 2
            if ipre < NPAIR * NOWN:
                b.dma("sync", xqs[ipre % 3][:], xqsc[ipre % NOWN],
                      s_xq[ipre % 3], waits=[(cT, xqs_free[ipre % 3])])
            xq_ck = s_xq[xslot].n
            for fc in range(16):
                wl = []
                if fc == 0:
                    wl = [(s_xq[xslot], xq_ck)]
                    if s == 0:
                        wl += [(cV, tern_ck), (cS, sck_t)]
                    if it < 3:
                        wl += [(cV, ck_wo_v), (cS, ck_wo_s)]
                    en, val = g_ps_free[pb]
                    wl.append((CTR[en], val))
                TE(lambda e, f=fc, r=ring, bb=pb, xs=xslot: e.matmul(
                    gps[bb][:], xqs[xs][:, f * 128:(f + 1) * 128],
                    terng[r][:, f * 512:(f + 1) * 512],
                    start=(f == 0), stop=(f == 15)), waits=wl, inc=(fc == 15))
            gck = cT.n
            for fc in range(16):
                wl = []
                if fc == 0:
                    if it < 3:
                        wl += [(cV, ck_wo_v), (cS, ck_wo_s)]
                    en, val = u_ps_free[pb]
                    wl.append((CTR[en], val))
                TE(lambda e, f=fc, r=ring, bb=pb, xs=xslot: e.matmul(
                    ups[bb][:], xqs[xs][:, f * 128:(f + 1) * 128],
                    ternu[r][:, f * 512:(f + 1) * 512],
                    start=(f == 0), stop=(f == 15)), waits=wl, inc=(fc == 15))
            uck = cT.n
            xqs_free[xslot] = uck
            if s == NOWN - 1:
                tern_free[ring] = uck
            sck = SC(lambda e, ss=s, bb=pb: e.activation(
                kg[:, 0:512], gps[bb][:], ACT.Silu, scale=rsg2[:, ss:ss + 1]),
                waits=[(cT, gck)])
            g_ps_free[pb] = ("cS", sck)
            vck = V(lambda e, ss=s, bb=pb: e.scalar_tensor_tensor(
                kg2[:, 0:512], ups[bb][:], rsg2u[:, ss:ss + 1], kg[:, 0:512],
                ALU.mult, ALU.mult), waits=[(cS, sck), (cT, uck)])
            u_ps_free[pb] = ("cV", vck)
            aslot = s % 2
            sck = SC(lambda e, a=aslot: e.activation(
                kg2[:, 512:1024], kg2[:, 0:512], ACT.Square,
                accum_out=acc_tmp[a][:]),
                waits=[(cV, vck), (cV, acc_free[aslot])])
            V(lambda e, ss=s, a=aslot: e.tensor_add(
                ssqh[:, ss:ss + 1], ssqh[:, ss:ss + 1], acc_tmp[a][:]),
              waits=[(cS, sck)])
            acc_free[aslot] = cV.n
            hslot = s % 2
            V(lambda e, gg=g, hs=hslot: e.tensor_tensor(
                hgb[hs][:], kg2[:, 0:512],
                dnw_s[:, gg * 512:(gg + 1) * 512], ALU.mult),
              waits=[(s_hgw[hslot], hgb_free[hslot]), (s_misc, ck_dnw)])
            V(lambda e, hs=hslot: e.reduce_max(lnst[:, 8:9], hgb[hs][:], axis=AX.X,
                                               apply_absolute_value=True))
            vck = V(lambda e, ss=s: e.tensor_tensor(
                mxh[:, ss:ss + 1], mxh[:, ss:ss + 1], lnst[:, 8:9], ALU.max))
            b.dma("sync", hgscr[s][:, g * 512:(g + 1) * 512], hgb[hslot][:],
                  s_hgw[hslot], waits=[(cV, vck)])
            hgb_free[hslot] = s_hgw[hslot].n

    V(lambda e: e.tensor_single_scalar(st1[:, 0:8], ssqh[:], 1.0 / INTER,
                                       ALU.mult))
    sck = SC(lambda e: e.activation(st1[:, 0:8], st1[:, 0:8], ACT.Sqrt, bias=EPS),
             waits=[(cV, cV.n)])
    V(lambda e: e.reciprocal(st1[:, 0:8], st1[:, 0:8]), waits=[(cS, sck)])
    V(lambda e: e.tensor_tensor(st2[:, 0:8], mxh[:], st1[:, 0:8], ALU.mult))
    V(lambda e: e.tensor_single_scalar(st2[:, 0:8], st2[:, 0:8], EPS, ALU.max))
    V(lambda e: e.tensor_single_scalar(rsd2[:], st2[:, 0:8], 1.0 / 127.0,
                                       ALU.mult))
    V(lambda e: e.tensor_single_scalar(rsd2[:], rsd2[:], hmd[:, 0:1], ALU.mult))
    V(lambda e: e.reciprocal(st3[:, 0:8], st2[:, 0:8]))
    V(lambda e: e.tensor_single_scalar(st3[:, 0:8], st3[:, 0:8], 127.0, ALU.mult))
    V(lambda e: e.tensor_tensor(qsh[:], st1[:, 0:8], st3[:, 0:8], ALU.mult))
    ck_qsh = cV.n

    b.wait("sync", s_hgw[0], s_hgw[1])
    b.barrier(bar)

    es_pg.close()
    es_ph4.close()
    es_t0.close()
    es_dt = ExitStack()
    dtern = [es_dt.enter_context(nc.sbuf_tensor(f"dtern{i}", [128, NFI * 512],
                                                BF16)) for i in range(2)]
    dtern_free = [0, 0]
    dtq = []
    for m in range(4):
        for f2 in range(NFI // 2):
            dtq.append((m, m % 2, f2))
    dtq_i = [0]
    dtern_ck_of = {}

    def emit_dtern(n, max_m=99):
        for _ in range(n):
            if dtq_i[0] >= len(dtq):
                return
            m, ring, f2 = dtq[dtq_i[0]]
            if m > max_m:
                return
            dtq_i[0] += 1
            extra = [(cT, dtern_free[ring])] if f2 == 0 else ()
            ck = tern_sc(dtern[ring][:, f2 * 1024:(f2 + 1) * 1024],
                         dwt_r, m * 22 + f2, thd_p, thd_m, extra=extra)
            if f2 == NFI // 2 - 1:
                dtern_ck_of[m] = ck

    es_4b = ExitStack()
    hgr = [es_4b.enter_context(nc.sbuf_tensor(f"hgr{i}", [128, 512], F32))
           for i in range(3)]
    q8b = es_4b.enter_context(nc.sbuf_tensor("q8b", [128, 512], I8))
    q8b2 = es_4b.enter_context(nc.sbuf_tensor("q8b2", [128, 512], I8))
    hgb4 = [es_4b.enter_context(nc.sbuf_tensor(f"hgb4_{i}", [128, 512], BF16))
            for i in range(3)]
    hqTb = [es_4b.enter_context(nc.sbuf_tensor(f"hqTb{i}", [128, INTER], BF16))
            for i in range(2)]
    es_p4b = ExitStack()
    tp4 = [es_p4b.enter_context(nc.psum_tensor(f"tp4_{i}", [128, 1024], BF16))
           for i in range(2)]
    hgr_free = [("cV", 0)] * 3
    hqTb_free = [0, 0]
    hgb_free = [("cT", 0)] * 3
    tp4_free = [("cV", 0)] * 4
    for s in range(NOWN):
        hb = s % 2
        b.wait("vector", (s_hqw[hb], hqTb_free[hb]))
        b.wait("scalar", (s_hqw[hb], hqTb_free[hb]))
        for jc in range(NPAIR):
            rs = (s * NPAIR + jc) % 3
            hs4 = (s * NPAIR + jc) % 3
            tp4s = jc % 4
            b.dma("sync", hgr[rs][:], hgscr[s][:, jc * 512:(jc + 1) * 512],
                  s_hgr[rs], waits=[(CTR[hgr_free[rs][0]], hgr_free[rs][1])])
            hen, hval = hgb_free[hs4]
            if jc % 2 == 0:
                vck1 = V(lambda e, r=rs, ss=s: e.tensor_scalar(
                    kg[:, 0:512], hgr[r][:], qsh[:, ss:ss + 1], CR, ALU.mult,
                    ALU.add), waits=[s_hgr[rs]])
                hgr_free[rs] = ("cV", vck1)
                V(lambda e: e.tensor_single_scalar(q8b[:], kg[:, 0:512],
                                                   CR, ALU.subtract))
                vck = V(lambda e, h4=hs4: e.tensor_copy(hgb4[h4][:], q8b[:]),
                        waits=[(CTR[hen], hval)])
                ev = ("cV", vck)
            else:
                sck1 = SC(lambda e, r=rs, ss=s: e.activation(
                    kg[:, 512:1024], hgr[r][:], ACT.Identity,
                    scale=qsh[:, ss:ss + 1], bias=CR),
                    waits=[s_hgr[rs]] + ([(cV, ck_qsh)]
                                         if (s == 0 and jc == 1) else []))
                hgr_free[rs] = ("cS", sck1)
                SC(lambda e: e.activation(q8b2[:], kg[:, 512:1024], ACT.Identity,
                                          bias=-CR))
                vck = SC(lambda e, h4=hs4: e.copy(hgb4[h4][:], q8b2[:]),
                         waits=[(CTR[hen], hval)])
                ev = ("cS", vck)
            hsrc = hgb4[hs4]
            tb = tp4s % 2
            th = tp4s // 2
            for j in range(4):
                wl = []
                if j == 0:
                    wl = [(CTR[ev[0]], ev[1])]
                    en, val = tp4_free[tp4s]
                    wl.append((CTR[en], val))
                TE(lambda e, jj=j, bb=tb, hh=th, hs_=hsrc: e.transpose(
                    tp4[bb][:, hh * 512 + jj * 128:hh * 512 + (jj + 1) * 128],
                    hs_[:, jj * 128:(jj + 1) * 128],
                    ident[:]), waits=wl, inc=(j == 3))
            tck = cT.n
            hgb_free[hs4] = ("cT", tck)
            wl2 = [(cT, tck)]
            if jc % 2 == 0:
                v = V(lambda e, j_=jc, hb_=hb, bb=tb, hh=th: e.tensor_copy(
                    hqTb[hb_][:, j_ * 512:(j_ + 1) * 512],
                    tp4[bb][:, hh * 512:(hh + 1) * 512]), waits=wl2)
                tp4_free[tp4s] = ("cV", v)
            else:
                v = SC(lambda e, j_=jc, hb_=hb, bb=tb, hh=th: e.copy(
                    hqTb[hb_][:, j_ * 512:(j_ + 1) * 512],
                    tp4[bb][:, hh * 512:(hh + 1) * 512]), waits=wl2)
                tp4_free[tp4s] = ("cS", v)
            if s < 2:
                emit_dtern(1, max_m=0)
        b.dma("gpsimd", hqtd[s], hqTb[hb][:], s_hqw[hb],
              waits=[(cV, cV.n), (cS, cS.n)])
        hqTb_free[hb] = s_hqw[hb].n

    b.wait("sync", s_hqw[0], s_hqw[1])
    b.barrier(bar)

    es_p4b.close()
    es_4b.close()
    es_4c = ExitStack()
    hqts = [es_4c.enter_context(nc.sbuf_tensor(f"hqts{i}", [128, INTER], BF16))
            for i in range(2)]
    xob = [es_4c.enter_context(nc.sbuf_tensor(f"xob{i}", [128, 512], F32))
           for i in range(2)]
    outb = [es_4c.enter_context(nc.sbuf_tensor(f"outb{i}", [128, 512], F32))
            for i in range(2)]
    es_p4c = ExitStack()
    dps = [es_p4c.enter_context(nc.psum_tensor(f"dps{i}", [128, 512], F32))
           for i in range(2)]
    hqts_free = [0, 0]
    d_ps_free = [("cV", 0), ("cV", 0)]
    xob_free = [0, 0]
    outb_free = [0, 0]
    it_list = [(m, s) for m in range(4) for s in range(NOWN)]
    hq_ck = [0, 0]
    b.dma("sync", hqts[0][:], hqtd[0], s_hqt[0], waits=[(cT, hqts_free[0])])
    hq_ck[0] = s_hqt[0].n
    for it_i, (m, s) in enumerate(it_list):
        ring = m % 2
        last_tern = dtern_ck_of[m]
        if True:
            hs = s % 2
            pb = s % 2
            emit_dtern(3, max_m=m + 1)
            if it_i + 1 < len(it_list):
                ns = it_list[it_i + 1][1]
                nhs = ns % 2
                b.dma("sync", hqts[nhs][:], hqtd[ns], s_hqt[nhs],
                      waits=[(cT, hqts_free[nhs])])
                hq_ck[nhs] = s_hqt[nhs].n
            b.dma("sync", xob[hs][:],
                  xown[s * 128:(s + 1) * 128, m * 512:(m + 1) * 512],
                  s_xo[hs], waits=[(cV, xob_free[hs])])
            hck = hq_ck[hs]
            xck = s_xo[hs].n
            for fc in range(NFI):
                wl = []
                if fc == 0:
                    wl = [(s_hqt[hs], hck)]
                    if s == 0:
                        wl.append((cV, last_tern))
                    en, val = d_ps_free[pb]
                    wl.append((CTR[en], val))
                TE(lambda e, f=fc, h_=hs, bb=pb, r=ring: e.matmul(
                    dps[bb][:], hqts[h_][:, f * 128:(f + 1) * 128],
                    dtern[r][:, f * 512:(f + 1) * 512],
                    start=(f == 0), stop=(f == NFI - 1)), waits=wl,
                   inc=(fc == NFI - 1))
            hqts_free[hs] = cT.n
            if s == NOWN - 1:
                dtern_free[ring] = cT.n
            mck = cT.n
            ob = s % 2
            vck = V(lambda e, ss=s, mm=m, bb=pb, o=ob: e.scalar_tensor_tensor(
                outb[o][:], dps[bb][:], rsd2[:, ss:ss + 1],
                attn_sb[:, ss * DM + mm * 512: ss * DM + (mm + 1) * 512],
                ALU.mult, ALU.add),
                waits=[(cT, mck), (s_out[ob], outb_free[ob])])
            d_ps_free[pb] = ("cV", cV.n)
            vck = V(lambda e, o=ob, h_=hs: e.tensor_add(
                outb[o][:], outb[o][:], xob[h_][:]), waits=[(s_xo[hs], xck)])
            xob_free[hs] = vck
            b.dma("sync", outd[s * 128:(s + 1) * 128, m * 512:(m + 1) * 512],
                  outb[ob][:], s_out[ob], waits=[(cV, vck)])
            outb_free[ob] = s_out[ob].n

    b.wait("sync", (s_out[0], s_out[0].n), (s_out[1], s_out[1].n))

    with nc.Block() as block:
        for name in ENGINES:
            fns = b.steps[name]
            getattr(block, name)(lambda eng, fns=fns: [f(eng) for f in fns])
    return nc


_NC_CACHE = None


def _chunk_major_gw(gw):
    gwt = np.ascontiguousarray(gw.T)
    g3 = gwt.reshape(8, 2, 128, 22, 512)
    arr = np.empty((128, 11, 8, 2, 2, 512), dtype=np.float32)
    arr[:, :, :, 0] = g3[:, :, :, 0:11].transpose(2, 3, 0, 1, 4)
    arr[:, :, :, 1] = g3[:, :, :, 11:22].transpose(2, 3, 0, 1, 4)
    return np.ascontiguousarray(arr.reshape(128, 176 * 1024))


def _chunk_major_dw(dw):
    dwt = np.ascontiguousarray(dw.T)
    d3 = dwt.reshape(22, 2, 128, 4, 512)
    dr = d3.transpose(2, 3, 0, 1, 4)
    return np.ascontiguousarray(dr).reshape(128, 88 * 1024)


def _host_prepare(inputs):
    x = np.asarray(inputs["L0_attn_hidden_states"], dtype=np.float32)
    Wq = np.asarray(inputs["L0_attn_Wq"], dtype=np.float32)
    Wk = np.asarray(inputs["L0_attn_Wk"], dtype=np.float32)
    Wv = np.asarray(inputs["L0_attn_Wv"], dtype=np.float32)
    Wo = np.asarray(inputs["L0_attn_Wo"], dtype=np.float32)
    gamma = np.asarray(inputs["L0_attn_gamma"], dtype=np.float32)
    beta = np.asarray(inputs["L0_attn_beta"], dtype=np.float32)
    gw = np.asarray(inputs["L0_ffn_gate_w"], dtype=np.float32)
    dw = np.asarray(inputs["L0_ffn_down_w"], dtype=np.float32)
    gnw = np.asarray(inputs["L0_ffn_gate_norm_w"], dtype=np.float32)
    dnw = np.asarray(inputs["L0_ffn_down_norm_w"], dtype=np.float32)

    bf = ml_dtypes.bfloat16
    shared = {
        "wqt": np.ascontiguousarray(Wq.T),
        "wkt": np.ascontiguousarray(Wk.T),
        "wvt": np.ascontiguousarray(Wv.T),
        "wot_r": np.ascontiguousarray(
            Wo.T.astype(np.float32).reshape(16, 128, 4, 512)
            .transpose(1, 2, 0, 3).reshape(128, 4 * 16 * 512)),
        "gwt_r": _chunk_major_gw(gw),
        "dwt_r": _chunk_major_dw(dw),
        "gbg": np.ascontiguousarray(np.broadcast_to(
            np.tile(gamma, H), (128, DM))).astype(bf),
        "gbb": np.ascontiguousarray(np.broadcast_to(
            np.tile(beta, H), (128, DM))).astype(bf),
        "gnwin": np.ascontiguousarray(np.broadcast_to(gnw, (128, DM))),
        "dnwin": np.ascontiguousarray(np.broadcast_to(dnw, (128, INTER))),
        "identin": np.eye(128, dtype=bf),
    }

    in_maps = []
    own_tiles_all = []
    for c in range(8):
        bidx, qq = c // 4, c % 4
        xb = x[bidx]
        own = list(range(4 * qq, 4 * qq + 4)) + list(range(28 - 4 * qq, 32 - 4 * qq))
        own_tiles_all.append(own)
        xo = np.concatenate([xb[tt * 128:(tt + 1) * 128] for tt in own], axis=0)
        masks = np.zeros((NT, 128, 512), dtype=np.float32)
        kidx = np.arange(128)
        qA = np.concatenate([tt * 128 + np.arange(128) for tt in own[:4]])
        qB = np.concatenate([tt * 128 + np.arange(128) for tt in own[4:]])
        for v in range(16):
            kk = v * 128 + kidx
            masks[v] = (qA[None, :] >= kk[:, None]).astype(np.float32)
        for v in range(16, 32):
            kk = v * 128 + kidx
            masks[v] = (qB[None, :] >= kk[:, None]).astype(np.float32)
        im = dict(shared)
        im["xfull"] = np.ascontiguousarray(xb)
        im["xown"] = np.ascontiguousarray(xo)
        im["masksin"] = masks.astype(bf)
        in_maps.append(im)
    return in_maps, own_tiles_all


def kernel(**inputs):
    global _NC_CACHE
    in_maps, own_tiles_all = _host_prepare(inputs)
    if _NC_CACHE is None:
        _NC_CACHE = build_nc()
    nc = _NC_CACHE
    res = run_bass_kernel_spmd(nc, in_maps, core_ids=list(range(8)))
    out = np.empty((2, T, DM), dtype=np.float32)
    for c in range(8):
        bidx = c // 4
        o = np.asarray(res.results[c]["outd"], dtype=np.float32)
        for i, tt in enumerate(own_tiles_all[c]):
            out[bidx, tt * 128:(tt + 1) * 128] = o[i * 128:(i + 1) * 128]
    return out
